# version 22 (final)
# baseline (speedup 1.0000x reference)
import numpy as np
import ml_dtypes
import concourse.bass as bass
import concourse.mybir as mybir

F32 = mybir.dt.float32
BF16 = mybir.dt.bfloat16
FP16 = mybir.dt.float16
HALF = FP16
NPHALF = ml_dtypes.float16 if hasattr(ml_dtypes, 'float16') else None
import numpy as _np
NPHALF = _np.float16
U8 = mybir.dt.uint8
AF = mybir.ActivationFunctionType
ALU = mybir.AluOpType
AX = mybir.AxisListType

D = 1024
NE = 16
NEXP = 17
DI = 512


class Sch:
    ENGS = ("pe", "act", "dve", "gp", "sync")

    def __init__(self, nc):
        self.nc = nc
        self.prog = {e: [] for e in self.ENGS}
        self.semnames = ["pe", "act", "dve", "gp_c", "dx", "do", "dwg0", "dwg1", "dwd0", "dwd1", "dc", "dcg", "dcv0", "dcv1", "dcd0", "dcd1", "dss"]
        self.ct = {s: 0 for s in self.semnames}
        self.pe_mode = None

    def eng_sem(self, eng):
        return {"pe": "pe", "act": "act", "dve": "dve", "gp": "gp_c"}.get(eng)

    def emit(self, eng, fn, waits=(), inc=None, drain_before=False):
        DMAS = ("dx", "do", "dwg0", "dwg1", "dwd0", "dwd1", "dc", "dcg", "dcv0", "dcv1", "dcd0", "dcd1", "dss")
        wl = {}
        own = self.eng_sem(eng)
        for w in waits:
            if w is None:
                continue
            s, v = w
            if s == own:
                if eng != "pe":
                    drain_before = True
                continue
            if not (self.ct[s] >= v >= 0):
                raise AssertionError(f"wait on future milestone {s}>={v} (now {self.ct[s]})")
            if s in DMAS:
                v = self.ct[s]
            wl[s] = max(wl.get(s, 0), v)
        mil = None
        if inc is True:
            s = own
            self.ct[s] += 1
            mil = (s, self.ct[s])
        elif inc:
            s, amt = inc
            self.ct[s] += amt
            mil = (s, self.ct[s])
        self.prog[eng].append(dict(waits=sorted(wl.items()), fn=fn, inc=mil,
                                   drain_before=drain_before))
        return mil

    PE_DRAIN_MODES = ("pv0", "pv1", "k32")

    def pe_filler(self, fn, n):
        self.prog["pe"].append(dict(waits=[], fn=lambda e, fn=fn, n=n: fn(e, n), inc=None,
                                    drain_before=False))

    def pe(self, mode, fn, waits=(), inc=None):
        db = (mode != self.pe_mode) and (
            mode in self.PE_DRAIN_MODES or self.pe_mode in self.PE_DRAIN_MODES)
        self.pe_mode = mode
        return self.emit("pe", fn, waits=waits, inc=inc, drain_before=db)

    def replay(self, ctx):
        nc = self.nc
        sems = {s: ctx.enter_context(nc.semaphore(f"sem_{s}")) for s in self.semnames}
        with nc.Block() as block:
            def body_for(ename):
                def body(eng):
                    for st_ in self.prog[ename]:
                        for s, v in st_["waits"]:
                            eng.wait_ge(sems[s], v)
                        if st_["drain_before"]:
                            eng.drain()
                        ret = st_["fn"](eng)
                        if st_["inc"] is not None:
                            s, v = st_["inc"]
                            amt = 16 if s != "pe" and s != "act" and s != "dve" and s != "gp_c" else 1
                            if ret is not None:
                                ret.then_inc(sems[s], amt)
                            else:
                                eng.sem_inc(sems[s], amt)
                return body
            block.sync(body_for("sync"))
            block.gpsimd(body_for("gp"))
            block.tensor(body_for("pe"))
            block.scalar(body_for("act"))
            block.vector(body_for("dve"))


def later(*mils):
    return [m for m in mils if m is not None]


def build(ntok, attn_only=False, skip_scratch=False):
    nc = bass.Bass("TRN2", target_bir_lowering=False)
    subs = []
    t = 0
    while ntok - t >= 384:
        if ntok - t == 512:
            subs.append((t, 2)); t += 256
            subs.append((t, 2)); t += 256
            break
        subs.append((t, 3)); t += 384
    if ntok - t == 256:
        subs.append((t, 2)); t += 256
    elif ntok - t == 128:
        subs.append((t, 1)); t += 128
    assert t == ntok, (t, ntok)

    xh = nc.dram_tensor("x", [ntok, D], F32, kind="ExternalInput")
    wqkv = nc.dram_tensor("wqkv", [D, 3 * D], F32, kind="ExternalInput")
    wo = nc.dram_tensor("wo", [D, D], F32, kind="ExternalInput")
    gatew = nc.dram_tensor("gatew", [D, NE], F32, kind="ExternalInput")
    swg = nc.dram_tensor("swg", [D, DI], F32, kind="ExternalInput")
    swu = nc.dram_tensor("swu", [D, DI], F32, kind="ExternalInput")
    swd = nc.dram_tensor("swd", [DI, D], F32, kind="ExternalInput")
    ewg = nc.dram_tensor("ewg", [NE, D, DI], F32, kind="ExternalInput")
    ewu = nc.dram_tensor("ewu", [NE, D, DI], F32, kind="ExternalInput")
    ewd = nc.dram_tensor("ewd", [NE, DI, D], F32, kind="ExternalInput")
    bqk = nc.dram_tensor("bqk", [128, 16], F32, kind="ExternalInput")
    bvrow = nc.dram_tensor("bvrow", [1, D], F32, kind="ExternalInput")
    borow = nc.dram_tensor("borow", [1, D], F32, kind="ExternalInput")
    gbrow = nc.dram_tensor("gbrow", [1, NE], F32, kind="ExternalInput")
    identh = nc.dram_tensor("identh", [128, 128], HALF, kind="ExternalInput")
    eallh = nc.dram_tensor("eallh", [16, 16 * 128], HALF, kind="ExternalInput")
    outh = nc.dram_tensor("out", [ntok, D], F32, kind="ExternalOutput")
    scrg = nc.dram_tensor("scrg", [NEXP, 128, 8 * DI], HALF)
    scru = nc.dram_tensor("scru", [NEXP, 128, 8 * DI], HALF)
    scrd = nc.dram_tensor("scrd", [NEXP, 128, 4 * D], HALF)

    from contextlib import ExitStack
    ctx = ExitStack()
    sb = lambda name, shape, dt: ctx.enter_context(nc.sbuf_tensor(name, shape, dt))

    MQ = 3
    MT = 128 * MQ
    x_tm = sb("x_tm", [128, MQ, D], F32)
    xa = sb("xa", [128, MQ, D], F32)
    z1_tm = sb("z1_tm", [128, MQ, D], HALF)
    z1_fm = sb("z1_fm", [128, 8, MT], HALF)
    psb = sb("psb", [128, 4, MT], HALF)
    pts = sb("pts", [128, 2, MT], HALF)
    wqkv_sb = sb("wqkv_sb", [128, 8, 3 * D], HALF)
    wo_sb = sb("wo_sb", [128, 8, D], HALF)
    gatew_sb = sb("gatew_sb", [128, 8, NE], HALF)
    wg_sb = sb("wg_sb", [128, 2, 8, DI], HALF)
    wu_sb = sb("wu_sb", [128, 2, 8, DI], HALF)
    wd_sb = sb("wd_sb", [128, 2, 4, D], HALF)
    ids = sb("ids", [128, 128], HALF)
    eall = sb("eall", [16, 16 * 128], HALF)
    bqk_sb = sb("bqk_sb", [128, 16], F32)
    bqk8 = sb("bqk8", [128, 8], F32)
    bvb = sb("bvb", [128, D], HALF)
    bob = sb("bob", [128, D], HALF)
    gbb = sb("gbb", [128, NE], F32)
    epsc = sb("epsc", [128, 1], F32)
    bnst = sb("bnst", [128, 2, 6], F32)
    mvs = sb("mvs", [128, 4, MQ, 2], F32)
    rstds = sb("rstds", [128, 4, MQ], F32)
    nmr = sb("nmr", [128, 4, MQ], F32)
    nmx = sb("nmx", [128, 4, 6], F32)
    ssum = sb("ssum", [128, 4, 6], F32)
    rr = sb("rr", [128, 4, 6], F32)
    gl = sb("gl", [128, MQ, NE], F32)
    ge = sb("ge", [128, MQ, NE], F32)
    m1 = sb("m1", [128, MQ], F32)
    m2 = sb("m2", [128, MQ], F32)
    den = sb("den", [128, MQ], F32)
    eqm = sb("eqm", [128, MQ, NE], F32)
    w_tm = sb("w_tm", [128, MQ, NE], F32)
    wsb = sb("wsb", [128, MQ, NE], HALF)
    wfm = sb("wfm", [16, MT], HALF)
    gs = sb("gs", [128, 2, 4, MT], HALF)
    us = sb("us", [128, 2, 4, MT], HALF)
    hs = sb("hs", [128, 2, 4, MT], HALF)
    slabA = sb("slabA", [128, 12288], U8)
    slabC = sb("slabC", [128, 8192], U8)
    of = sb("of", [128, 8, MT], HALF)

    def view(slab, off_bytes, dt, shape):
        ap = slab.ap().bitcast(dt)
        esz = mybir.dt.size(dt)
        n = 1
        for s in shape[1:]:
            n *= s
        v = ap[:, off_bytes // esz: off_bytes // esz + n]
        if len(shape) == 3:
            v = v.rearrange("p (a b) -> p a b", b=shape[2])
        return v

    qf = view(slabA, 0, HALF, [128, 8, MT])
    kf = view(slabA, 6144, HALF, [128, 8, MT])
    wbc = view(slabA, 0, FP16, [128, NE, MT])
    vt = view(slabC, 0, HALF, [128, MQ, D])
    aps = view(slabC, 0, F32, [128, 1, D])
    z2t = view(slabC, 4096, F32, [128, 1, D])

    PSUMT = ctx.enter_context(nc.psum_tensor("PSA", [128, 8, 512], F32))
    PS = PSUMT.ap()
    PSB3 = PSUMT.ap().bitcast(HALF)

    k = Sch(nc)
    bank_w = [None] * 8
    bank_r = [[] for _ in range(8)]

    def bank_write_waits(b):
        ws = list(bank_r[b]) + later(bank_w[b])
        bank_r[b] = []
        return ws

    def bank_set_w(b, mil):
        bank_w[b] = mil

    def bank_read(b, mil):
        bank_r[b].append(mil)

    m_eps = k.emit("gp", lambda e: e.memset(epsc[:, :], 1e-5), inc=True)
    cst = [
        k.emit("sync", lambda e: e.dma_start(out=ids[:, :], in_=identh[:, :]), inc=("dc", 16)),
        k.emit("sync", lambda e: e.dma_start(out=eall[:, :], in_=eallh[:, :]), inc=("dc", 16)),
        k.emit("sync", lambda e: e.dma_start(out=bqk_sb[:, :], in_=bqk[:, :]), inc=("dc", 16)),
        k.emit("gp", lambda e: e.dma_start(out=bvb[:, :], in_=bvrow.ap().broadcast_to([128, D])), inc=("dcg", 16)),
        k.emit("gp", lambda e: e.dma_start(out=bob[:, :], in_=borow.ap().broadcast_to([128, D])), inc=("dcg", 16)),
        k.emit("sync", lambda e: e.dma_start(out=gbb[:, :], in_=gbrow.ap().broadcast_to([128, NE])), inc=("dc", 16)),
    ]
    m_bqk8 = k.emit("dve", lambda e: e.tensor_scalar(
        out=bqk8[:, :], in0=bqk_sb[:, 0:8], scalar1=0.125, scalar2=None,
        op0=ALU.mult, op1=ALU.bypass), waits=later(cst[2]), inc=True)
    wq_r = wqkv.rearrange("(k p) f -> p k f", p=128)
    m_wqkv = []
    for h in range(4):
        m_wqkv.append(k.emit("gp", lambda e, h=h: e.dma_start(
            out=wqkv_sb[:, 2 * h:2 * h + 2, :], in_=wq_r[:, 2 * h:2 * h + 2, :]), inc=("dcg", 16)))
    m_wqkv = m_wqkv[-1]
    m_wo = k.emit("gp", lambda e: e.dma_start(
        out=wo_sb[:, :, :], in_=wo.rearrange("(k p) f -> p k f", p=128)), inc=("dcg", 16))
    m_gw = k.emit("gp", lambda e: e.dma_start(
        out=gatew_sb[:, :, :], in_=gatew.rearrange("(k p) f -> p k f", p=128)), inc=("dcg", 16))

    conv_gu_store = [None] * NEXP
    conv_d_store = [None] * NEXP
    conv_gu_by_buf = {0: None, 1: None}
    conv_d_by_buf = {0: None, 1: None}

    def load_gu(e_i, buf, waits):
        waits = waits + later(conv_gu_by_buf[buf])
        if conv_gu_store[e_i] is None:
            g_src = swg if e_i == 0 else ewg[e_i - 1]
            u_src = swu if e_i == 0 else ewu[e_i - 1]
            dsem = "dcv%d" % buf
            k.emit("gp", lambda e, s=g_src, b=buf: e.dma_start(
                out=wg_sb[:, b, :, :], in_=s.rearrange("(k p) f -> p k f", p=128)),
                waits=waits, inc=(dsem, 16))
            lu = k.emit("gp", lambda e, s=u_src, b=buf: e.dma_start(
                out=wu_sb[:, b, :, :], in_=s.rearrange("(k p) f -> p k f", p=128)), inc=(dsem, 16))
            k.emit("sync", lambda e, b=buf, e_i=e_i: e.dma_start(
                out=scrg[e_i], in_=wg_sb[:, b, :, :].rearrange("p a b -> p (a b)")),
                waits=later(lu), inc=("dss", 16))
            stu = k.emit("sync", lambda e, b=buf, e_i=e_i: e.dma_start(
                out=scru[e_i], in_=wu_sb[:, b, :, :].rearrange("p a b -> p (a b)")),
                waits=later(lu), inc=("dss", 16))
            conv_gu_store[e_i] = stu
            conv_gu_by_buf[buf] = stu
            return lu
        conv_gu_by_buf[buf] = None
        sem = "dwg%d" % buf
        k.emit("act", lambda e, e_i=e_i, b=buf: e.dma_start(
            out=wg_sb[:, b, :, :].rearrange("p a b -> p (a b)"), in_=scrg[e_i]),
            waits=waits + later(conv_gu_store[e_i]), inc=(sem, 16))
        lu = k.emit("act", lambda e, e_i=e_i, b=buf: e.dma_start(
            out=wu_sb[:, b, :, :].rearrange("p a b -> p (a b)"), in_=scru[e_i]), inc=(sem, 16))
        return lu

    def load_d(e_i, buf, waits):
        waits = waits + later(conv_d_by_buf[buf])
        if conv_d_store[e_i] is None:
            d_src = swd if e_i == 0 else ewd[e_i - 1]
            ld = k.emit("gp", lambda e, s=d_src, b=buf: e.dma_start(
                out=wd_sb[:, b, :, :], in_=s.rearrange("(k p) f -> p k f", p=128)),
                waits=waits, inc=("dcd%d" % buf, 16))
            std = k.emit("sync", lambda e, b=buf, e_i=e_i: e.dma_start(
                out=scrd[e_i], in_=wd_sb[:, b, :, :].rearrange("p a b -> p (a b)")),
                waits=later(ld), inc=("dss", 16))
            conv_d_store[e_i] = std
            conv_d_by_buf[buf] = std
            return ld
        conv_d_by_buf[buf] = None
        ld = k.emit("sync", lambda e, e_i=e_i, b=buf: e.dma_start(
            out=wd_sb[:, b, :, :].rearrange("p a b -> p (a b)"), in_=scrd[e_i]),
            waits=waits + later(conv_d_store[e_i]), inc=("dwd%d" % buf, 16))
        return ld

    const_ready = later(*cst)
    prev_od = None
    prev_x2_mils = []
    prev_ytr = []
    prev_b4x3 = None
    exm_mil = {}
    tr_mil = {}
    pt_evac = {}
    pv_mil = {}
    wbuf_last_use = [None, None]
    wgu_loaded = [None, None]
    wd_loaded = [None, None]
    if not attn_only:
        wgu_loaded[0] = load_gu(0, 0, [])
        wd_loaded[0] = load_d(0, 0, [])
        wgu_loaded[1] = load_gu(1, 1, [])
        wd_loaded[1] = load_d(1, 1, [])

    def emit_head_x(sub_j):
        T0, NQ = subs[sub_j]
        return k.emit("sync", lambda e, T0=T0, NQ=NQ: e.dma_start(
            out=xa[:, 0:NQ, :],
            in_=xh[T0:T0 + 128 * NQ, :].rearrange("(q p) d -> p q d", p=128)),
            waits=later(*prev_x2_mils), inc=("dx", 16))

    def emit_head_q(xin, q):
        k.emit("dve", lambda e, q=q: e.bn_stats(bnst[:, 0, :], xa[:, q, 0:512]),
               waits=later(xin), drain_before=True)
        k.emit("dve", lambda e, q=q: e.bn_stats(bnst[:, 1, :], xa[:, q, 512:1024]))
        sm_ = k.emit("dve", lambda e, q=q: e.bn_aggr(
            mvs[:, 0, q, :], bnst.ap().rearrange("p a b -> p (a b)")),
            drain_before=True, inc=True)
        ln_m = k.emit("act", lambda e, q=q: e.activation(
            out=rstds[:, 0, q:q + 1], in_=mvs[:, 0, q, 1:2], func=AF.Ln,
            bias=epsc[:, 0:1], scale=1.0), waits=later(sm_, m_eps), inc=True)
        rs_m = k.emit("act", lambda e, q=q: e.activation(
            out=rstds[:, 0, q:q + 1], in_=rstds[:, 0, q:q + 1], func=AF.Exp, scale=-0.5),
            waits=later(ln_m), drain_before=True, inc=True)
        nm_ = k.emit("dve", lambda e, q=q: e.tensor_scalar(
            out=nmr[:, 0, q:q + 1], in0=mvs[:, 0, q, 0:1], scalar1=rstds[:, 0, q:q + 1],
            scalar2=-1.0, op0=ALU.mult, op1=ALU.mult),
            waits=later(rs_m), drain_before=True, inc=True)
        am = k.emit("gp", lambda e, q=q: e.tensor_scalar(
            out=z1_tm[:, q, :], in0=xa[:, q, :],
            scalar1=rstds[:, 0, q:q + 1], scalar2=nmr[:, 0, q:q + 1],
            op0=ALU.mult, op1=ALU.add),
            waits=later(nm_) + later(*prev_ytr), drain_before=True, inc=True)
        return am

    def emit_head(sub_j):
        xin = emit_head_x(sub_j)
        return [emit_head_q(xin, q) for q in range(subs[sub_j][1])]

    head_mils = emit_head(0)

    for sub_i, (T0, NQ) in enumerate(subs):
        ST = 128 * NQ
        NSEG = ST // 64

        def ln_stats_apply(src_view, dst_view, ready_by_q, stage, nq=NQ, extra_app_waits=()):
            app = []
            for q in range(nq):
                k.emit("dve", lambda e, q=q: e.bn_stats(bnst[:, 0, :], src_view[:, q, 0:512]),
                       waits=later(ready_by_q[min(q, len(ready_by_q) - 1)]), drain_before=True)
                k.emit("dve", lambda e, q=q: e.bn_stats(bnst[:, 1, :], src_view[:, q, 512:1024]))
                sm_ = k.emit("dve", lambda e, q=q, stage=stage: e.bn_aggr(
                    mvs[:, stage, q, :], bnst.ap().rearrange("p a b -> p (a b)")),
                    drain_before=True, inc=True)
                ln_m = k.emit("act", lambda e, q=q, stage=stage: e.activation(
                    out=rstds[:, stage, q:q + 1], in_=mvs[:, stage, q, 1:2], func=AF.Ln,
                    bias=epsc[:, 0:1], scale=1.0), waits=later(sm_, m_eps), inc=True)
                rs_m = k.emit("act", lambda e, q=q, stage=stage: e.activation(
                    out=rstds[:, stage, q:q + 1], in_=rstds[:, stage, q:q + 1],
                    func=AF.Exp, scale=-0.5), waits=later(ln_m), drain_before=True, inc=True)
                nm_ = k.emit("dve", lambda e, q=q, stage=stage: e.tensor_scalar(
                    out=nmr[:, stage, q:q + 1], in0=mvs[:, stage, q, 0:1],
                    scalar1=rstds[:, stage, q:q + 1], scalar2=-1.0,
                    op0=ALU.mult, op1=ALU.mult),
                    waits=later(rs_m), drain_before=True, inc=True)
                am = k.emit("gp", lambda e, q=q, stage=stage: e.tensor_scalar(
                    out=dst_view[:, q, :], in0=src_view[:, q, :],
                    scalar1=rstds[:, stage, q:q + 1], scalar2=nmr[:, stage, q:q + 1],
                    op0=ALU.mult, op1=ALU.add),
                    waits=later(nm_) + list(extra_app_waits), drain_before=True, inc=True)
                app.append(am)
            return app

        z1_mils = head_mils

        def transpose_tm_to_fm(src_tm, dst_fm, src_mils, nq=NQ, ret_tr=False):
            evac = []
            trs = []
            g = 0
            for q in range(nq):
                for ch in range(2):
                    b = g % 2
                    ws = later(src_mils[q]) + bank_write_waits(b) + const_ready
                    def tr(e, q=q, ch=ch, b=b):
                        last = None
                        for cc in range(4):
                            c = 4 * ch + cc
                            last = e.transpose(PSB3[:, b, 128 * cc:128 * cc + 128],
                                               src_tm[:, q, 128 * c:128 * c + 128], ids[:, :])
                        return last
                    tm_ = k.pe("tr", tr, waits=ws, inc=True)
                    bank_set_w(b, tm_)
                    trs.append(tm_)
                    def ev(e, q=q, ch=ch, b=b):
                        dst = dst_fm[:, 4 * ch:4 * ch + 4, 128 * q:128 * q + 128]
                        return e.activation(
                            out=dst, in_=PSB3[:, b, 0:512].rearrange("p (a c) -> p a c", c=128),
                            func=AF.Identity, bias=0.0, scale=1.0)
                    em_ = k.emit("act", ev, waits=later(tm_), inc=True)
                    bank_read(b, em_)
                    evac.append(em_)
                    g += 1
            if ret_tr:
                return evac, trs
            return evac

        z1fm_mils = transpose_tm_to_fm(z1_tm, z1_fm, z1_mils)
        z1fm_all = later(z1fm_mils[-1], *z1fm_mils)

        def qkcols(kc, c, which):
            off = 1024 * which + 128 * c
            return wqkv_sb[:, kc, off:off + 128]

        qk_mils = []
        gi = 0
        for c in range(8):
            for which in range(2):
                b = 2 + gi % 4
                ws = z1fm_all + bank_write_waits(b) + later(m_wqkv)
                def mmq(e, c=c, b=b, which=which, ST=ST):
                    last = None
                    for kc in range(8):
                        last = e.matmul(PS[:, b, 0:ST], qkcols(kc, c, which),
                                        z1_fm[:, kc, 0:ST], start=(kc == 0), stop=(kc == 7))
                    return last
                pm = k.pe("f", mmq, waits=ws, inc=True)
                bank_set_w(b, pm)
                dst = qf if which == 0 else kf
                if which == 0:
                    em = k.emit("act", lambda e, c=c, b=b, dst=dst, ST=ST: e.activation(
                        out=dst[:, c, :ST], in_=PS[:, b, 0:ST], func=AF.Identity,
                        bias=bqk8[:, c:c + 1], scale=0.125),
                        waits=later(pm, m_bqk8), inc=True)
                else:
                    em = k.emit("act", lambda e, c=c, b=b, dst=dst, ST=ST: e.activation(
                        out=dst[:, c, :ST], in_=PS[:, b, 0:ST], func=AF.Identity,
                        bias=bqk_sb[:, 8 + c:8 + c + 1], scale=1.0),
                        waits=later(pm), inc=True)
                bank_read(b, em)
                qk_mils.append(em)
                gi += 1
        qk_ready = later(*qk_mils)

        v_mils = []
        for q in range(NQ):
            for half in range(2):
                b = 2 + gi % 4
                ws = z1fm_all + bank_write_waits(b) + later(m_wqkv)
                def mmv(e, q=q, half=half, b=b):
                    last = None
                    for kc in range(8):
                        rhs = wqkv_sb[:, kc, 2048 + 512 * half:2048 + 512 * half + 512]
                        last = e.matmul(PS[:, b, :], z1_fm[:, kc, 128 * q:128 * q + 128], rhs,
                                        start=(kc == 0), stop=(kc == 7))
                    return last
                pm = k.pe("f", mmv, waits=ws, inc=True)
                bank_set_w(b, pm)
                em = k.emit("dve", lambda e, q=q, half=half, b=b: e.scalar_tensor_tensor(
                    out=vt[:, q, 512 * half:512 * half + 512], in0=PS[:, b, :],
                    scalar=1.0, in1=bvb[:, 512 * half:512 * half + 512],
                    op0=ALU.mult, op1=ALU.add),
                    waits=later(pm) + (later(prev_b4x3) if (q == 0 and half == 0) else []),
                    inc=True, drain_before=(q == 0 and half == 0))
                bank_read(b, em)
                v_mils.append(em)
                gi += 1
        v_ready = later(*v_mils)

        of_mils = []
        S_banks = [2, 3, 6, 7]
        s_mil = {}
        p_mil = {}

        def emit_S(c):
            b = S_banks[c % 4]
            ws = qk_ready + bank_write_waits(b)
            def mms(e, c=c, b=b, NSEG=NSEG):
                last = None
                for s in range(NSEG):
                    e.matmul(PS[0:64, b, 64 * s:64 * s + 64],
                             qf[0:64, c, 64 * s:64 * s + 64], kf[0:64, c, 64 * s:64 * s + 64],
                             start=True, stop=True, tile_position=(0, 0))
                    last = e.matmul(PS[64:128, b, 64 * s:64 * s + 64],
                                    qf[64:128, c, 64 * s:64 * s + 64], kf[64:128, c, 64 * s:64 * s + 64],
                                    start=True, stop=True, tile_position=(64, 64))
                return last
            pm = k.pe("q64", mms, waits=ws, inc=True)
            bank_set_w(b, pm)
            s_mil[c] = pm

        def emit_softmax(c):
            ci = c % 4
            b = S_banks[c % 4]
            ws = later(s_mil[c]) + later(exm_mil.get((c - 4) % 8))
            rm = k.emit("dve", lambda e, ci=ci, b=b, NSEG=NSEG, ST=ST: e.tensor_reduce(
                nmx[:, ci, 0:NSEG], PS[:, b, 0:ST].rearrange("p (s t) -> p s t", t=64),
                axis=AX.X, op=ALU.max, negate=True), waits=ws, drain_before=True, inc=True)
            exm = None
            psb_war = later(tr_mil.get((c - 4) % 8))
            for s in range(NSEG):
                exm = k.emit("act", lambda e, ci=ci, b=b, s=s: e.activation(
                    out=psb[:, ci, 64 * s:64 * s + 64], in_=PS[:, b, 64 * s:64 * s + 64],
                    func=AF.Exp, bias=nmx[:, ci, s:s + 1], scale=1.0),
                    waits=later(rm) + (psb_war if s == 0 else []),
                    inc=(True if s == NSEG - 1 else None))
            exm_mil[c] = exm
            bank_read(b, exm)
            sm_ = k.emit("dve", lambda e, ci=ci, NSEG=NSEG, ST=ST: e.tensor_reduce(
                ssum[:, ci, 0:NSEG], psb[:, ci, 0:ST].rearrange("p (s t) -> p s t", t=64),
                axis=AX.X, op=ALU.add), waits=later(exm), inc=True)
            rc = k.emit("dve", lambda e, ci=ci, NSEG=NSEG: e.reciprocal(
                rr[:, ci, 0:NSEG], ssum[:, ci, 0:NSEG]), waits=later(sm_),
                drain_before=True, inc=True)
            pm_ = None
            for s in range(NSEG):
                pm_ = k.emit("dve", lambda e, ci=ci, s=s: e.tensor_scalar(
                    out=psb[:, ci, 64 * s:64 * s + 64], in0=psb[:, ci, 64 * s:64 * s + 64],
                    scalar1=rr[:, ci, s:s + 1], scalar2=None, op0=ALU.mult, op1=ALU.bypass),
                    waits=later(rc), drain_before=(s == 0), inc=(True if s == NSEG - 1 else None))
            p_mil[c] = pm_

        def emit_T_pe(c):
            psrc = c % 4
            b = c % 2
            ws = later(p_mil[c]) + bank_write_waits(b)
            def trp(e, psrc=psrc, b=b, NQ=NQ):
                last = None
                for blk in range(NQ):
                    last = e.transpose(PSB3[:, b, 128 * blk:128 * blk + 128],
                                       psb[:, psrc, 128 * blk:128 * blk + 128], ids[:, :])
                return last
            tm_ = k.pe("tr", trp, waits=ws, inc=True)
            bank_set_w(b, tm_)
            tr_mil[c] = tm_

        def emit_T_evac(c):
            pci = c % 2
            b = c % 2
            ws2 = later(tr_mil[c]) + later(pv_mil.get((c - 2) % 8))
            em = k.emit("dve", lambda e, pci=pci, b=b, ST=ST: e.tensor_copy(
                pts[:, pci, 0:ST], PSB3[:, b, 0:ST]), waits=ws2, inc=True)
            bank_read(b, em)
            pt_evac[c] = em

        def mpv(e, c, ci, b, parity, NSEG):
            last = None
            for s in range(parity, NSEG, 2):
                sp = s % 2
                tq = s // 2
                for hp in range(2):
                    last = e.matmul(
                        PS[64 * hp:64 * hp + 64, b, 64 * s:64 * s + 64],
                        vt[64 * sp:64 * sp + 64, tq, 128 * c + 64 * hp:128 * c + 64 * hp + 64],
                        pts[64 * sp:64 * sp + 64, ci,
                            128 * (s // 2) + 64 * hp:128 * (s // 2) + 64 * hp + 64],
                        start=True, stop=True, tile_position=(64 * sp, 64 * hp))
            return last

        def emit_PV_parity(cs, parity):
            last = {}
            for c in cs:
                ci = c % 2
                b = 4 + ci
                ws = later(pt_evac[c]) + v_ready
                if parity == 0:
                    ws += bank_write_waits(b)
                last[c] = k.pe("pv%d" % parity,
                               lambda e, c=c, ci=ci, b=b, parity=parity, NSEG=NSEG: mpv(
                                   e, c, ci, b, parity, NSEG),
                               waits=ws, inc=(True if parity == 1 else None))
            return last

        def emit_O_evac(c):
            b = 4 + c % 2
            pvm = pv_mil[c]
            bank_set_w(b, pvm)
            em = k.emit("act", lambda e, c=c, b=b, ST=ST: e.activation(
                out=of[:, c, 0:ST], in_=PS[:, b, 0:ST], func=AF.Identity, bias=0.0, scale=1.0),
                waits=later(pvm), inc=True)
            bank_read(b, em)
            of_mils.append(em)

        emit_S(0)
        emit_softmax(0)
        emit_S(1)
        emit_softmax(1)
        emit_S(2)
        emit_S(3)
        for pr in range(4):
            c0, c1 = 2 * pr, 2 * pr + 1
            emit_T_pe(c0)
            emit_T_pe(c1)
            emit_T_evac(c0)
            emit_T_evac(c1)
            for cn in (c0 + 2, c1 + 2):
                if cn < 8 and cn >= 4:
                    pass
            if c0 + 2 < 8 and c0 + 2 >= 2:
                if c0 + 2 >= 2 and (c0 + 2) not in p_mil:
                    emit_softmax(c0 + 2)
                if c1 + 2 < 8 and (c1 + 2) not in p_mil:
                    emit_softmax(c1 + 2)
            emit_PV_parity((c0, c1), 0)
            pv2 = emit_PV_parity((c0, c1), 1)
            pv_mil[c0] = pv2[c0]
            pv_mil[c1] = pv2[c1]
            emit_O_evac(c0)
            emit_O_evac(c1)
            for cn in (2 * pr + 4, 2 * pr + 5):
                if cn < 8:
                    emit_S(cn)
        of_ready = later(*of_mils)

        x2_mils = []
        prev_resadd = None
        for q in range(NQ):
            bpair = (6, 7) if q % 2 == 0 else (2, 3)
            ap_mils = []
            for jh in range(2):
                b = bpair[jh]
                ws = of_ready + bank_write_waits(b) + later(m_wo)
                def mmo(e, q=q, jh=jh, b=b):
                    last = None
                    for c in range(8):
                        last = e.matmul(PS[:, b, :], of[:, c, 128 * q:128 * q + 128],
                                        wo_sb[:, c, 512 * jh:512 * jh + 512],
                                        start=(c == 0), stop=(c == 7))
                    return last
                pm = k.pe("f", mmo, waits=ws, inc=True)
                bank_set_w(b, pm)
                em = k.emit("dve", lambda e, jh=jh, b=b: e.scalar_tensor_tensor(
                    out=aps[:, 0, 512 * jh:512 * jh + 512], in0=PS[:, b, :], scalar=1.0,
                    in1=bob[:, 512 * jh:512 * jh + 512], op0=ALU.mult, op1=ALU.add),
                    waits=later(pm), drain_before=(jh == 0), inc=True)
                bank_read(b, em)
                ap_mils.append(em)
            k.emit("dve", lambda e: e.bn_stats(bnst[:, 0, :], aps[:, 0, 0:512]),
                   waits=later(*ap_mils), drain_before=True)
            k.emit("dve", lambda e: e.bn_stats(bnst[:, 1, :], aps[:, 0, 512:1024]))
            ag = k.emit("dve", lambda e, q=q: e.bn_aggr(
                mvs[:, 1, q, :], bnst.ap().rearrange("p a b -> p (a b)")), drain_before=True, inc=True)
            ln_m = k.emit("act", lambda e, q=q: e.activation(
                out=rstds[:, 1, q:q + 1], in_=mvs[:, 1, q, 1:2], func=AF.Ln, bias=epsc[:, 0:1], scale=1.0),
                waits=later(ag, m_eps), inc=True)
            rs_m = k.emit("act", lambda e, q=q: e.activation(
                out=rstds[:, 1, q:q + 1], in_=rstds[:, 1, q:q + 1], func=AF.Exp, scale=-0.5),
                waits=later(ln_m), drain_before=True, inc=True)
            ts_m = k.emit("dve", lambda e, q=q: e.tensor_scalar(
                out=z2t[:, 0, :], in0=aps[:, 0, :], scalar1=mvs[:, 1, q, 0:1],
                scalar2=rstds[:, 1, q:q + 1], op0=ALU.subtract, op1=ALU.mult),
                waits=later(rs_m, prev_resadd), drain_before=True, inc=True)
            x2m = k.emit("dve", lambda e, q=q: e.tensor_tensor(
                out=x_tm[:, q, :], in0=xa[:, q, :], in1=z2t[:, 0, :], op=ALU.add),
                waits=later(ts_m) + [("do", k.ct["do"])], drain_before=True, inc=True)
            prev_resadd = x2m
            x2_mils.append(x2m)

        if attn_only:
            od = k.emit("sync", lambda e, T0=T0, NQ=NQ: e.dma_start(
                out=outh[T0:T0 + 128 * NQ, :].rearrange("(q p) d -> p q d", p=128),
                in_=x_tm[:, 0:NQ, :]), waits=later(*x2_mils), inc=("do", 16))
            prev_x2_mils = x2_mils
            prev_ytr = []
            if sub_i + 1 < len(subs):
                head_mils = emit_head(sub_i + 1)
            continue

        y_mils = ln_stats_apply(x_tm, z1_tm, x2_mils, 2)
        yfm_mils, ytr_mils = transpose_tm_to_fm(z1_tm, z1_fm, y_mils, ret_tr=True)
        yfm_all = later(yfm_mils[-1], *yfm_mils)

        sig_mils = []
        for q in range(NQ):
            b = 6 + q % 2
            ws = yfm_all + bank_write_waits(b) + later(m_gw)
            def mmg(e, q=q, b=b):
                last = None
                for kc in range(8):
                    last = e.matmul(PS[:, b, 0:NE], z1_fm[:, kc, 128 * q:128 * q + 128],
                                    gatew_sb[:, kc, :], start=(kc == 0), stop=(kc == 7))
                return last
            pm = k.pe("f", mmg, waits=ws, inc=True)
            bank_set_w(b, pm)
            em = k.emit("dve", lambda e, q=q, b=b: e.scalar_tensor_tensor(
                out=gl[:, q, :], in0=PS[:, b, 0:NE], scalar=1.0, in1=gbb[:, :],
                op0=ALU.mult, op1=ALU.add), waits=later(pm), inc=True)
            bank_read(b, em)
            ex = k.emit("act", lambda e, q=q: e.activation(
                out=ge[:, q, :], in_=gl[:, q, :], func=AF.Exp, scale=-1.0),
                waits=later(em), inc=True)
            t1 = k.emit("dve", lambda e, q=q: e.tensor_scalar(
                out=ge[:, q, :], in0=ge[:, q, :], scalar1=1.0, scalar2=None,
                op0=ALU.add, op1=ALU.bypass), waits=later(ex), inc=True)
            sg = k.emit("dve", lambda e, q=q: e.reciprocal(gl[:, q, :], ge[:, q, :]),
                        waits=later(t1), drain_before=True, inc=True)
            sig_mils.append(sg)

        wsb_mils = []
        for q in range(NQ):
            k.emit("dve", lambda e, q=q: e.tensor_reduce(
                m1[:, q:q + 1], gl[:, q, :], axis=AX.X, op=ALU.max),
                waits=later(sig_mils[q]), drain_before=True)
            k.emit("dve", lambda e, q=q: e.tensor_scalar(
                out=eqm[:, q, :], in0=gl[:, q, :], scalar1=m1[:, q:q + 1], scalar2=-1e30,
                op0=ALU.is_ge, op1=ALU.mult), drain_before=True)
            k.emit("dve", lambda e, q=q: e.tensor_tensor(
                out=eqm[:, q, :], in0=gl[:, q, :], in1=eqm[:, q, :], op=ALU.add), drain_before=True)
            k.emit("dve", lambda e, q=q: e.tensor_reduce(
                m2[:, q:q + 1], eqm[:, q, :], axis=AX.X, op=ALU.max), drain_before=True)
            k.emit("dve", lambda e, q=q: e.tensor_tensor(
                out=den[:, q:q + 1], in0=m1[:, q:q + 1], in1=m2[:, q:q + 1], op=ALU.add),
                drain_before=True)
            k.emit("dve", lambda e, q=q: e.reciprocal(den[:, q:q + 1], den[:, q:q + 1]),
                   drain_before=True)
            k.emit("dve", lambda e, q=q: e.tensor_scalar(
                out=eqm[:, q, :], in0=gl[:, q, :], scalar1=m2[:, q:q + 1], scalar2=None,
                op0=ALU.is_ge, op1=ALU.bypass), drain_before=True)
            k.emit("dve", lambda e, q=q: e.tensor_tensor(
                out=w_tm[:, q, :], in0=gl[:, q, :], in1=eqm[:, q, :], op=ALU.mult),
                drain_before=True)
            k.emit("dve", lambda e, q=q: e.tensor_scalar(
                out=w_tm[:, q, :], in0=w_tm[:, q, :], scalar1=den[:, q:q + 1], scalar2=None,
                op0=ALU.mult, op1=ALU.bypass), drain_before=True)
            wm = k.emit("dve", lambda e, q=q: e.tensor_copy(wsb[:, q, :], w_tm[:, q, :]),
                        drain_before=True, inc=True)
            wsb_mils.append(wm)
        h_prev = None
        h_mil = [None] * NEXP
        down_mil = [None] * NEXP
        up_last = [None] * NEXP
        wd_mil = [None] * NEXP
        last_down = {}
        acc_banks = {}
        acc_first = {}
        acc_war = {}

        def emit_ups(eidx):
            nonlocal h_prev
            wb_i = eidx % 2
            wm = wgu_loaded[wb_i]
            ev_g = []
            for i in range(4):
                b = i % 2
                ws = yfm_all + bank_write_waits(b) + later(wm)
                def mmug(e, i=i, b=b, wb_i=wb_i, ST=ST):
                    last = None
                    for kc in range(8):
                        last = e.matmul(PS[:, b, 0:ST], wg_sb[:, wb_i, kc, 128 * i:128 * i + 128],
                                        z1_fm[:, kc, 0:ST], start=(kc == 0), stop=(kc == 7))
                    return last
                pm = k.pe("f", mmug, waits=ws, inc=True)
                bank_set_w(b, pm)
                em = k.emit("act", lambda e, i=i, b=b, hp_=eidx % 2, ST=ST: e.activation(
                    out=gs[:, hp_, i, 0:ST], in_=PS[:, b, 0:ST], func=AF.Silu, bias=0.0, scale=1.0),
                    waits=later(pm, h_mil[eidx - 2] if eidx >= 2 else None), inc=True)
                bank_read(b, em)
                ev_g.append(em)
            ev_u = []
            for i in range(4):
                b = i % 2
                ws = yfm_all + bank_write_waits(b) + later(wm)
                def mmuu(e, i=i, b=b, wb_i=wb_i, ST=ST):
                    last = None
                    for kc in range(8):
                        last = e.matmul(PS[:, b, 0:ST], wu_sb[:, wb_i, kc, 128 * i:128 * i + 128],
                                        z1_fm[:, kc, 0:ST], start=(kc == 0), stop=(kc == 7))
                    return last
                pm = k.pe("f", mmuu, waits=ws, inc=True)
                bank_set_w(b, pm)
                up_last[eidx] = pm
                war_u = h_mil[eidx - 2] if eidx >= 2 else None
                if eidx == 0:
                    em = k.emit("dve", lambda e, i=i, b=b, hp_=eidx % 2, ST=ST: e.tensor_copy(
                        us[:, hp_, i, 0:ST], PS[:, b, 0:ST]), waits=later(pm, war_u), inc=True)
                else:
                    em = k.emit("dve", lambda e, i=i, b=b, hp_=eidx % 2, eidx=eidx, ST=ST: e.tensor_tensor(
                        out=us[:, hp_, i, 0:ST], in0=PS[:, b, 0:ST], in1=wbc[:, eidx - 1, 0:ST],
                        op=ALU.mult),
                        waits=later(pm, war_u, wbc_mils[eidx - 1]), inc=True)
                bank_read(b, em)
                ev_u.append(em)
            h_m = None
            hp_ = eidx % 2
            war = later(down_mil[eidx - 2]) if eidx >= 2 else []
            for i in range(4):
                h_m = k.emit("dve", lambda e, i=i, hp_=hp_, ST=ST: e.tensor_tensor(
                    out=hs[:, hp_, i, 0:ST], in0=gs[:, hp_, i, 0:ST], in1=us[:, hp_, i, 0:ST],
                    op=ALU.mult),
                    waits=later(ev_g[i], ev_u[i]) + (war if i == 0 else []),
                    drain_before=(i == 0), inc=(True if i == 3 else None))
            h_mil[eidx] = h_m
            h_prev = h_m

        def emit_downs(eidx):
            wb_i = eidx % 2
            hp_ = eidx % 2
            dm = None
            wdm = wd_loaded[wb_i]
            for tq in range(NQ):
                for jh in range(2):
                    bb = acc_banks[(tq, jh)]
                    ws = later(h_mil[eidx], wdm) + (acc_war[bb] if acc_first[bb] else [])
                    def mmd(e, tq=tq, jh=jh, bb=bb, wb_i=wb_i, hp_=hp_,
                            first=acc_first[bb], eidx=eidx):
                        last = None
                        for i in range(4):
                            last = e.matmul(PS[:, bb, :], hs[:, hp_, i, 128 * tq:128 * tq + 128],
                                            wd_sb[:, wb_i, i, 512 * jh:512 * jh + 512],
                                            start=(first and i == 0),
                                            stop=(eidx == NEXP - 1 and i == 3))
                        return last
                    dm = k.pe("f", mmd, waits=ws,
                              inc=(True if (tq == NQ - 1 and jh == 1) or eidx == NEXP - 1 else None))
                    if eidx == NEXP - 1:
                        bank_set_w(bb, dm)
                        last_down[(tq, jh)] = dm
                    acc_first[bb] = False
            down_mil[eidx] = dm

        prev_x2_mils = x2_mils
        prev_ytr = ytr_mils
        wbc_mils = []
        for eidx in range(NEXP):
            emit_ups(eidx)
            if eidx == 0:
                b = 0
                ws = later(*wsb_mils) + bank_write_waits(b)
                def trw(e, b=b, NQ=NQ):
                    last = None
                    for q in range(NQ):
                        last = e.transpose(PSB3[0:16, b, 128 * q:128 * q + 128],
                                           wsb[:, q, :], ids[:, :])
                    return last
                tw = k.pe("tr", trw, waits=ws, inc=True)
                bank_set_w(b, tw)
                ew = k.emit("dve", lambda e, b=b, ST=ST: e.tensor_copy(
                    wfm[0:16, 0:ST], PSB3[0:16, b, 0:ST]), waits=later(tw), inc=True)
                bank_read(b, ew)
                del wbc_mils[:]
                slabA_free = later(*[m for m in of_mils])
                for ei2 in range(NE):
                    b = 6 + ei2 % 2
                    ws = later(ew) + bank_write_waits(b) + (slabA_free if ei2 < 2 else [])
                    pm = k.pe("k32", lambda e, ei2=ei2, b=b, ST=ST: e.matmul(
                        PS[:, b, 0:ST], eall[0:16, 128 * ei2:128 * ei2 + 128], wfm[0:16, 0:ST],
                        start=True, stop=True), waits=ws, inc=True)
                    bank_set_w(b, pm)
                    em = k.emit("act", lambda e, ei2=ei2, b=b, ST=ST: e.activation(
                        out=wbc[:, ei2, 0:ST], in_=PS[:, b, 0:ST], func=AF.Identity,
                        bias=0.0, scale=1.0), waits=later(pm), inc=True)
                    bank_read(b, em)
                    wbc_mils.append(em)
                wbc_ready = later(*wbc_mils)
                acc_banks.clear(); acc_first.clear(); acc_war.clear()
                for tq in range(NQ):
                    for jh in range(2):
                        bb = 2 + 2 * tq + jh
                        acc_banks[(tq, jh)] = bb
                for bb in set(acc_banks.values()):
                    acc_first[bb] = True
                    acc_war[bb] = bank_write_waits(bb)
            if eidx + 2 < NEXP:
                wgu_loaded[eidx % 2] = load_gu(eidx + 2, eidx % 2, later(up_last[eidx]))
            if eidx > 0:
                emit_downs(eidx - 1)
                if eidx + 1 < NEXP:
                    wd_loaded[(eidx + 1) % 2] = load_d(eidx + 1, (eidx + 1) % 2,
                                                       later(down_mil[eidx - 1]))
            if sub_i + 1 < len(subs):
                if eidx == 8:
                    head_xin = emit_head_x(sub_i + 1)
                    head_mils = []
                elif eidx in (10, 12, 14):
                    q = (eidx - 10) // 2
                    if q < subs[sub_i + 1][1]:
                        head_mils.append(emit_head_q(head_xin, q))
        emit_downs(NEXP - 1)
        if sub_i + 1 < len(subs):
            wgu_loaded[0] = load_gu(0, 0, later(up_last[NEXP - 1]))
            wgu_loaded[1] = load_gu(1, 1, later(up_last[NEXP - 2]))
            wd_loaded[1] = load_d(1, 1, later(down_mil[NEXP - 2]))
            wd_loaded[0] = load_d(0, 0, later(down_mil[NEXP - 1]))
        if len(subs) == 1 or (sub_i + 1 == len(subs)):
            pass

        ffn_done = []
        prev_x3 = prev_resadd
        for tq in range(NQ):
            yrc = k.emit("dve", lambda e, tq=tq: e.tensor_scalar(
                out=z2t[:, 0, :], in0=x_tm[:, tq, :], scalar1=mvs[:, 2, tq, 0:1],
                scalar2=rstds[:, 2, tq:tq + 1], op0=ALU.subtract, op1=ALU.mult),
                waits=later(last_down[(tq, 0)], prev_x3), drain_before=True, inc=True)
            apm = []
            for jh in range(2):
                bb = acc_banks[(tq, jh)]
                em = k.emit("dve", lambda e, tq=tq, jh=jh, bb=bb: e.scalar_tensor_tensor(
                    out=aps[:, 0, 512 * jh:512 * jh + 512], in0=PS[:, bb, :], scalar=1.0,
                    in1=z2t[:, 0, 512 * jh:512 * jh + 512], op0=ALU.mult, op1=ALU.add),
                    waits=later(last_down[(tq, jh)], yrc), drain_before=True, inc=True)
                bank_read(bb, em)
                apm.append(em)
            k.emit("dve", lambda e: e.bn_stats(bnst[:, 0, :], aps[:, 0, 0:512]),
                   waits=later(*apm), drain_before=True)
            k.emit("dve", lambda e: e.bn_stats(bnst[:, 1, :], aps[:, 0, 512:1024]))
            ag = k.emit("dve", lambda e, tq=tq: e.bn_aggr(
                mvs[:, 3, tq, :], bnst.ap().rearrange("p a b -> p (a b)")), drain_before=True, inc=True)
            ln_m = k.emit("act", lambda e, tq=tq: e.activation(
                out=rstds[:, 3, tq:tq + 1], in_=mvs[:, 3, tq, 1:2], func=AF.Ln, bias=epsc[:, 0:1], scale=1.0),
                waits=later(ag, m_eps), inc=True)
            rs_m = k.emit("act", lambda e, tq=tq: e.activation(
                out=rstds[:, 3, tq:tq + 1], in_=rstds[:, 3, tq:tq + 1], func=AF.Exp, scale=-0.5),
                waits=later(ln_m), drain_before=True, inc=True)
            ts_m = k.emit("dve", lambda e, tq=tq: e.tensor_scalar(
                out=z2t[:, 0, :], in0=aps[:, 0, :], scalar1=mvs[:, 3, tq, 0:1],
                scalar2=rstds[:, 3, tq:tq + 1], op0=ALU.subtract, op1=ALU.mult),
                waits=later(rs_m), drain_before=True, inc=True)
            x3m = k.emit("dve", lambda e, tq=tq: e.tensor_tensor(
                out=x_tm[:, tq, :], in0=x_tm[:, tq, :], in1=z2t[:, 0, :], op=ALU.add),
                waits=later(ts_m), drain_before=True, inc=True)
            prev_x3 = x3m
            prev_b4x3 = x3m
            ffn_done.append(x3m)
        od = k.emit("sync", lambda e, T0=T0, NQ=NQ: e.dma_start(
            out=outh[T0:T0 + 128 * NQ, :].rearrange("(q p) d -> p q d", p=128),
            in_=x_tm[:, 0:NQ, :]), waits=later(*ffn_done), inc=("do", 16))

    k.emit("sync", lambda e: None, waits=[(s, k.ct[s]) for s in
                                          ("do", "dx", "dwg0", "dwg1", "dwd0", "dwd1", "dc", "dcg", "dcv0", "dcv1", "dcd0", "dcd1", "dss")])
    k.replay(ctx)
    ctx.close()
    return nc


from concourse.bass_utils import run_bass_kernel_spmd


def _permute_wqkv(w_qkv):
    w = np.asarray(w_qkv, np.float32)
    idx = np.zeros(3 * D, np.int64)
    for c in range(8):
        for j in range(128):
            h = 2 * c + (1 if j >= 64 else 0)
            m = j % 64
            idx[128 * c + j] = h * 192 + m
            idx[1024 + 128 * c + j] = h * 192 + 64 + m
    for h in range(16):
        for m in range(64):
            idx[2048 + 64 * h + m] = h * 192 + 128 + m
    return np.ascontiguousarray(w[:, idx])


def _host_layout(b_qkv, b_o, gate_b):
    bqk = np.zeros((128, 16), np.float32)
    b_qkv = np.asarray(b_qkv, np.float32)
    for c in range(8):
        for p in range(128):
            h = 2 * c + (1 if p >= 64 else 0)
            m = p % 64
            bqk[p, c] = b_qkv[h * 192 + m]
            bqk[p, 8 + c] = b_qkv[h * 192 + 64 + m]
    bvrow = np.zeros((1, D), np.float32)
    for h in range(16):
        bvrow[0, 64 * h:64 * h + 64] = b_qkv[h * 192 + 128:h * 192 + 192]
    eallv = np.zeros((16, 2048), np.float32)
    for e in range(16):
        eallv[e, 128 * e:128 * (e + 1)] = 1.0
    return dict(
        bqk=bqk, bvrow=bvrow,
        borow=np.asarray(b_o, np.float32).reshape(1, D),
        gbrow=np.asarray(gate_b, np.float32).reshape(1, NE),
        identh=np.eye(128).astype(NPHALF),
        eallh=eallv.astype(NPHALF),
    )


def kernel(L0_attn_x, L0_attn_w_qkv, L0_attn_b_qkv, L0_attn_w_o, L0_attn_b_o,
           L0_ffn_gate_w, L0_ffn_gate_b, L0_ffn_shared_wg, L0_ffn_shared_wu,
           L0_ffn_shared_wd, L0_ffn_experts_wg, L0_ffn_experts_wu, L0_ffn_experts_wd):
    x = np.ascontiguousarray(np.asarray(L0_attn_x, np.float32))
    B, S, Dm = x.shape
    assert (B, S, Dm) == (8, 4096, 1024)
    consts = _host_layout(L0_attn_b_qkv, L0_attn_b_o, L0_ffn_gate_b)
    shared = dict(
        wqkv=_permute_wqkv(L0_attn_w_qkv),
        wo=np.ascontiguousarray(np.asarray(L0_attn_w_o, np.float32)),
        gatew=np.ascontiguousarray(np.asarray(L0_ffn_gate_w, np.float32)),
        swg=np.ascontiguousarray(np.asarray(L0_ffn_shared_wg, np.float32)),
        swu=np.ascontiguousarray(np.asarray(L0_ffn_shared_wu, np.float32)),
        swd=np.ascontiguousarray(np.asarray(L0_ffn_shared_wd, np.float32)),
        ewg=np.ascontiguousarray(np.asarray(L0_ffn_experts_wg, np.float32)),
        ewu=np.ascontiguousarray(np.asarray(L0_ffn_experts_wu, np.float32)),
        ewd=np.ascontiguousarray(np.asarray(L0_ffn_experts_wd, np.float32)),
        **consts)
    nc = build(S)
    in_maps = [dict(x=x[c], **shared) for c in range(8)]
    res = run_bass_kernel_spmd(nc, in_maps, core_ids=list(range(8)))
    return np.stack([r["out"] for r in res.results], axis=0)


# version 23
# speedup vs baseline: 1.0093x; 1.0093x over previous
import numpy as np
import ml_dtypes
import concourse.bass as bass
import concourse.mybir as mybir

F32 = mybir.dt.float32
BF16 = mybir.dt.bfloat16
FP16 = mybir.dt.float16
HALF = FP16
NPHALF = ml_dtypes.float16 if hasattr(ml_dtypes, 'float16') else None
import numpy as _np
NPHALF = _np.float16
U8 = mybir.dt.uint8
AF = mybir.ActivationFunctionType
ALU = mybir.AluOpType
AX = mybir.AxisListType

D = 1024
NE = 16
NEXP = 17
DI = 512


class Sch:
    ENGS = ("pe", "act", "dve", "gp", "sync")

    def __init__(self, nc):
        self.nc = nc
        self.prog = {e: [] for e in self.ENGS}
        self.semnames = ["pe", "act", "dve", "gp_c", "dx", "do", "dwg0", "dwg1", "dwd0", "dwd1", "dc", "dcg", "dcv0", "dcv1", "dcd0", "dcd1", "dss"]
        self.ct = {s: 0 for s in self.semnames}
        self.pe_mode = None

    def eng_sem(self, eng):
        return {"pe": "pe", "act": "act", "dve": "dve", "gp": "gp_c"}.get(eng)

    def emit(self, eng, fn, waits=(), inc=None, drain_before=False):
        DMAS = ("dx", "do", "dwg0", "dwg1", "dwd0", "dwd1", "dc", "dcg", "dcv0", "dcv1", "dcd0", "dcd1", "dss")
        wl = {}
        own = self.eng_sem(eng)
        for w in waits:
            if w is None:
                continue
            s, v = w
            if s == own:
                if eng != "pe":
                    drain_before = True
                continue
            if not (self.ct[s] >= v >= 0):
                raise AssertionError(f"wait on future milestone {s}>={v} (now {self.ct[s]})")
            if s in DMAS:
                v = self.ct[s]
            wl[s] = max(wl.get(s, 0), v)
        mil = None
        if inc is True:
            s = own
            self.ct[s] += 1
            mil = (s, self.ct[s])
        elif inc:
            s, amt = inc
            self.ct[s] += amt
            mil = (s, self.ct[s])
        self.prog[eng].append(dict(waits=sorted(wl.items()), fn=fn, inc=mil,
                                   drain_before=drain_before))
        return mil

    PE_DRAIN_MODES = ("pv0", "pv1", "k32")

    def pe_filler(self, fn, n):
        self.prog["pe"].append(dict(waits=[], fn=lambda e, fn=fn, n=n: fn(e, n), inc=None,
                                    drain_before=False))

    def pe(self, mode, fn, waits=(), inc=None):
        db = (mode != self.pe_mode) and (
            mode in self.PE_DRAIN_MODES or self.pe_mode in self.PE_DRAIN_MODES)
        self.pe_mode = mode
        return self.emit("pe", fn, waits=waits, inc=inc, drain_before=db)

    def replay(self, ctx):
        nc = self.nc
        sems = {s: ctx.enter_context(nc.semaphore(f"sem_{s}")) for s in self.semnames}
        with nc.Block() as block:
            def body_for(ename):
                def body(eng):
                    for st_ in self.prog[ename]:
                        for s, v in st_["waits"]:
                            eng.wait_ge(sems[s], v)
                        if st_["drain_before"]:
                            eng.drain()
                        ret = st_["fn"](eng)
                        if st_["inc"] is not None:
                            s, v = st_["inc"]
                            amt = 16 if s != "pe" and s != "act" and s != "dve" and s != "gp_c" else 1
                            if ret is not None:
                                ret.then_inc(sems[s], amt)
                            else:
                                eng.sem_inc(sems[s], amt)
                return body
            block.sync(body_for("sync"))
            block.gpsimd(body_for("gp"))
            block.tensor(body_for("pe"))
            block.scalar(body_for("act"))
            block.vector(body_for("dve"))


def later(*mils):
    return [m for m in mils if m is not None]


def build(ntok, attn_only=False, skip_scratch=False):
    nc = bass.Bass("TRN2", target_bir_lowering=False)
    subs = []
    t = 0
    while ntok - t >= 384:
        if ntok - t == 512:
            subs.append((t, 2)); t += 256
            subs.append((t, 2)); t += 256
            break
        subs.append((t, 3)); t += 384
    if ntok - t == 256:
        subs.append((t, 2)); t += 256
    elif ntok - t == 128:
        subs.append((t, 1)); t += 128
    assert t == ntok, (t, ntok)

    xh = nc.dram_tensor("x", [ntok, D], F32, kind="ExternalInput")
    wqkv = nc.dram_tensor("wqkv", [D, 3 * D], F32, kind="ExternalInput")
    wo = nc.dram_tensor("wo", [D, D], F32, kind="ExternalInput")
    gatew = nc.dram_tensor("gatew", [D, NE], F32, kind="ExternalInput")
    swg = nc.dram_tensor("swg", [D, DI], F32, kind="ExternalInput")
    swu = nc.dram_tensor("swu", [D, DI], F32, kind="ExternalInput")
    swd = nc.dram_tensor("swd", [DI, D], F32, kind="ExternalInput")
    ewg = nc.dram_tensor("ewg", [NE, D, DI], F32, kind="ExternalInput")
    ewu = nc.dram_tensor("ewu", [NE, D, DI], F32, kind="ExternalInput")
    ewd = nc.dram_tensor("ewd", [NE, DI, D], F32, kind="ExternalInput")
    bqk = nc.dram_tensor("bqk", [128, 16], F32, kind="ExternalInput")
    bvrow = nc.dram_tensor("bvrow", [1, D], F32, kind="ExternalInput")
    borow = nc.dram_tensor("borow", [1, D], F32, kind="ExternalInput")
    gbrow = nc.dram_tensor("gbrow", [1, NE], F32, kind="ExternalInput")
    identh = nc.dram_tensor("identh", [128, 128], HALF, kind="ExternalInput")
    eallh = nc.dram_tensor("eallh", [16, 16 * 128], HALF, kind="ExternalInput")
    outh = nc.dram_tensor("out", [ntok, D], F32, kind="ExternalOutput")
    scrg = nc.dram_tensor("scrg", [NEXP, 128, 8 * DI], HALF)
    scru = nc.dram_tensor("scru", [NEXP, 128, 8 * DI], HALF)
    scrd = nc.dram_tensor("scrd", [NEXP, 128, 4 * D], HALF)

    from contextlib import ExitStack
    ctx = ExitStack()
    sb = lambda name, shape, dt: ctx.enter_context(nc.sbuf_tensor(name, shape, dt))

    MQ = 3
    MT = 128 * MQ
    x_tm = sb("x_tm", [128, MQ, D], F32)
    xa = sb("xa", [128, MQ, D], F32)
    z1_tm = sb("z1_tm", [128, MQ, D], HALF)
    z1_fm = sb("z1_fm", [128, 8, MT], HALF)
    psb = sb("psb", [128, 4, MT], HALF)
    pts = sb("pts", [128, 2, MT], HALF)
    wqkv_sb = sb("wqkv_sb", [128, 8, 3 * D], HALF)
    wo_sb = sb("wo_sb", [128, 8, D], HALF)
    gatew_sb = sb("gatew_sb", [128, 8, NE], HALF)
    wg_sb = sb("wg_sb", [128, 2, 8, DI], HALF)
    wu_sb = sb("wu_sb", [128, 2, 8, DI], HALF)
    wd_sb = sb("wd_sb", [128, 2, 4, D], HALF)
    ids = sb("ids", [128, 128], HALF)
    eall = sb("eall", [16, 16 * 128], HALF)
    bqk_sb = sb("bqk_sb", [128, 16], F32)
    bqk8 = sb("bqk8", [128, 8], F32)
    bvb = sb("bvb", [128, D], HALF)
    bob = sb("bob", [128, D], HALF)
    gbb = sb("gbb", [128, NE], F32)
    epsc = sb("epsc", [128, 1], F32)
    bnst = sb("bnst", [128, 2, 6], F32)
    mvs = sb("mvs", [128, 4, MQ, 2], F32)
    rstds = sb("rstds", [128, 4, MQ], F32)
    nmr = sb("nmr", [128, 4, MQ], F32)
    nmx = sb("nmx", [128, 4, 6], F32)
    ssum = sb("ssum", [128, 4, 6], F32)
    rr = sb("rr", [128, 4, 6], F32)
    gl = sb("gl", [128, MQ, NE], F32)
    ge = sb("ge", [128, MQ, NE], F32)
    m1 = sb("m1", [128, MQ], F32)
    m2 = sb("m2", [128, MQ], F32)
    den = sb("den", [128, MQ], F32)
    eqm = sb("eqm", [128, MQ, NE], F32)
    w_tm = sb("w_tm", [128, MQ, NE], F32)
    wsb = sb("wsb", [128, MQ, NE], HALF)
    wfm = sb("wfm", [16, MT], HALF)
    gs = sb("gs", [128, 2, 4, MT], HALF)
    us = sb("us", [128, 2, 4, MT], HALF)
    hs = sb("hs", [128, 2, 4, MT], HALF)
    slabA = sb("slabA", [128, 12288], U8)
    slabC = sb("slabC", [128, 8192], U8)
    of = sb("of", [128, 8, MT], HALF)

    def view(slab, off_bytes, dt, shape):
        ap = slab.ap().bitcast(dt)
        esz = mybir.dt.size(dt)
        n = 1
        for s in shape[1:]:
            n *= s
        v = ap[:, off_bytes // esz: off_bytes // esz + n]
        if len(shape) == 3:
            v = v.rearrange("p (a b) -> p a b", b=shape[2])
        return v

    qf = view(slabA, 0, HALF, [128, 8, MT])
    kf = view(slabA, 6144, HALF, [128, 8, MT])
    wbc = view(slabA, 0, FP16, [128, NE, MT])
    vt = view(slabC, 0, HALF, [128, MQ, D])
    aps = view(slabC, 0, F32, [128, 1, D])
    z2t = view(slabC, 4096, F32, [128, 1, D])

    PSUMT = ctx.enter_context(nc.psum_tensor("PSA", [128, 8, 512], F32))
    PS = PSUMT.ap()
    PSB3 = PSUMT.ap().bitcast(HALF)

    k = Sch(nc)
    bank_w = [None] * 8
    bank_r = [[] for _ in range(8)]

    def bank_write_waits(b):
        ws = list(bank_r[b]) + later(bank_w[b])
        bank_r[b] = []
        return ws

    def bank_set_w(b, mil):
        bank_w[b] = mil

    def bank_read(b, mil):
        bank_r[b].append(mil)

    m_eps = k.emit("gp", lambda e: e.memset(epsc[:, :], 1e-5), inc=True)
    cst = [
        k.emit("sync", lambda e: e.dma_start(out=ids[:, :], in_=identh[:, :]), inc=("dc", 16)),
        k.emit("sync", lambda e: e.dma_start(out=eall[:, :], in_=eallh[:, :]), inc=("dc", 16)),
        k.emit("sync", lambda e: e.dma_start(out=bqk_sb[:, :], in_=bqk[:, :]), inc=("dc", 16)),
        k.emit("gp", lambda e: e.dma_start(out=bvb[:, :], in_=bvrow.ap().broadcast_to([128, D])), inc=("dcg", 16)),
        k.emit("gp", lambda e: e.dma_start(out=bob[:, :], in_=borow.ap().broadcast_to([128, D])), inc=("dcg", 16)),
        k.emit("sync", lambda e: e.dma_start(out=gbb[:, :], in_=gbrow.ap().broadcast_to([128, NE])), inc=("dc", 16)),
    ]
    m_bqk8 = k.emit("dve", lambda e: e.tensor_scalar(
        out=bqk8[:, :], in0=bqk_sb[:, 0:8], scalar1=0.125, scalar2=None,
        op0=ALU.mult, op1=ALU.bypass), waits=later(cst[2]), inc=True)
    wq_r = wqkv.rearrange("(k p) f -> p k f", p=128)
    m_wqkv = []
    for h in range(4):
        m_wqkv.append(k.emit("gp", lambda e, h=h: e.dma_start(
            out=wqkv_sb[:, 2 * h:2 * h + 2, :], in_=wq_r[:, 2 * h:2 * h + 2, :]), inc=("dcg", 16)))
    m_wqkv = m_wqkv[-1]
    m_wo = k.emit("gp", lambda e: e.dma_start(
        out=wo_sb[:, :, :], in_=wo.rearrange("(k p) f -> p k f", p=128)), inc=("dcg", 16))
    m_gw = k.emit("gp", lambda e: e.dma_start(
        out=gatew_sb[:, :, :], in_=gatew.rearrange("(k p) f -> p k f", p=128)), inc=("dcg", 16))

    conv_gu_store = [None] * NEXP
    conv_d_store = [None] * NEXP
    conv_gu_by_buf = {0: None, 1: None}
    conv_d_by_buf = {0: None, 1: None}

    def load_gu(e_i, buf, waits):
        waits = waits + later(conv_gu_by_buf[buf])
        if conv_gu_store[e_i] is None:
            g_src = swg if e_i == 0 else ewg[e_i - 1]
            u_src = swu if e_i == 0 else ewu[e_i - 1]
            dsem = "dcv%d" % buf
            k.emit("gp", lambda e, s=g_src, b=buf: e.dma_start(
                out=wg_sb[:, b, :, :], in_=s.rearrange("(k p) f -> p k f", p=128)),
                waits=waits, inc=(dsem, 16))
            lu = k.emit("gp", lambda e, s=u_src, b=buf: e.dma_start(
                out=wu_sb[:, b, :, :], in_=s.rearrange("(k p) f -> p k f", p=128)), inc=(dsem, 16))
            k.emit("sync", lambda e, b=buf, e_i=e_i: e.dma_start(
                out=scrg[e_i], in_=wg_sb[:, b, :, :].rearrange("p a b -> p (a b)")),
                waits=later(lu), inc=("dss", 16))
            stu = k.emit("sync", lambda e, b=buf, e_i=e_i: e.dma_start(
                out=scru[e_i], in_=wu_sb[:, b, :, :].rearrange("p a b -> p (a b)")),
                waits=later(lu), inc=("dss", 16))
            conv_gu_store[e_i] = stu
            conv_gu_by_buf[buf] = stu
            return lu
        conv_gu_by_buf[buf] = None
        sem = "dwg%d" % buf
        k.emit("act", lambda e, e_i=e_i, b=buf: e.dma_start(
            out=wg_sb[:, b, :, :].rearrange("p a b -> p (a b)"), in_=scrg[e_i]),
            waits=waits + later(conv_gu_store[e_i]), inc=(sem, 16))
        lu = k.emit("act", lambda e, e_i=e_i, b=buf: e.dma_start(
            out=wu_sb[:, b, :, :].rearrange("p a b -> p (a b)"), in_=scru[e_i]), inc=(sem, 16))
        return lu

    def load_d(e_i, buf, waits):
        waits = waits + later(conv_d_by_buf[buf])
        if conv_d_store[e_i] is None:
            d_src = swd if e_i == 0 else ewd[e_i - 1]
            ld = k.emit("gp", lambda e, s=d_src, b=buf: e.dma_start(
                out=wd_sb[:, b, :, :], in_=s.rearrange("(k p) f -> p k f", p=128)),
                waits=waits, inc=("dcd%d" % buf, 16))
            std = k.emit("sync", lambda e, b=buf, e_i=e_i: e.dma_start(
                out=scrd[e_i], in_=wd_sb[:, b, :, :].rearrange("p a b -> p (a b)")),
                waits=later(ld), inc=("dss", 16))
            conv_d_store[e_i] = std
            conv_d_by_buf[buf] = std
            return ld
        conv_d_by_buf[buf] = None
        ld = k.emit("sync", lambda e, e_i=e_i, b=buf: e.dma_start(
            out=wd_sb[:, b, :, :].rearrange("p a b -> p (a b)"), in_=scrd[e_i]),
            waits=waits + later(conv_d_store[e_i]), inc=("dwd%d" % buf, 16))
        return ld

    const_ready = later(*cst)
    prev_od = None
    prev_x2_mils = []
    prev_ytr = []
    prev_b4x3 = None
    exm_mil = {}
    tr_mil = {}
    pt_evac = {}
    pv_mil = {}
    wbuf_last_use = [None, None]
    wgu_loaded = [None, None]
    wd_loaded = [None, None]
    if not attn_only:
        wgu_loaded[0] = load_gu(0, 0, [])
        wd_loaded[0] = load_d(0, 0, [])
        wgu_loaded[1] = load_gu(1, 1, [])
        wd_loaded[1] = load_d(1, 1, [])

    def emit_head_x(sub_j):
        T0, NQ = subs[sub_j]
        return k.emit("sync", lambda e, T0=T0, NQ=NQ: e.dma_start(
            out=xa[:, 0:NQ, :],
            in_=xh[T0:T0 + 128 * NQ, :].rearrange("(q p) d -> p q d", p=128)),
            waits=later(*prev_x2_mils), inc=("dx", 16))

    def emit_head_q(xin, q):
        k.emit("dve", lambda e, q=q: e.bn_stats(bnst[:, 0, :], xa[:, q, 0:512]),
               waits=later(xin), drain_before=True)
        k.emit("dve", lambda e, q=q: e.bn_stats(bnst[:, 1, :], xa[:, q, 512:1024]))
        sm_ = k.emit("dve", lambda e, q=q: e.bn_aggr(
            mvs[:, 0, q, :], bnst.ap().rearrange("p a b -> p (a b)")),
            drain_before=True, inc=True)
        ln_m = k.emit("act", lambda e, q=q: e.activation(
            out=rstds[:, 0, q:q + 1], in_=mvs[:, 0, q, 1:2], func=AF.Ln,
            bias=epsc[:, 0:1], scale=1.0), waits=later(sm_, m_eps), inc=True)
        rs_m = k.emit("act", lambda e, q=q: e.activation(
            out=rstds[:, 0, q:q + 1], in_=rstds[:, 0, q:q + 1], func=AF.Exp, scale=-0.5),
            waits=later(ln_m), drain_before=True, inc=True)
        nm_ = k.emit("dve", lambda e, q=q: e.tensor_scalar(
            out=nmr[:, 0, q:q + 1], in0=mvs[:, 0, q, 0:1], scalar1=rstds[:, 0, q:q + 1],
            scalar2=-1.0, op0=ALU.mult, op1=ALU.mult),
            waits=later(rs_m), drain_before=True, inc=True)
        am = k.emit("gp", lambda e, q=q: e.tensor_scalar(
            out=z1_tm[:, q, :], in0=xa[:, q, :],
            scalar1=rstds[:, 0, q:q + 1], scalar2=nmr[:, 0, q:q + 1],
            op0=ALU.mult, op1=ALU.add),
            waits=later(nm_) + later(*prev_ytr), drain_before=True, inc=True)
        return am

    def emit_head(sub_j):
        xin = emit_head_x(sub_j)
        return [emit_head_q(xin, q) for q in range(subs[sub_j][1])]

    head_mils = emit_head(0)
    pending_b4end = None

    for sub_i, (T0, NQ) in enumerate(subs):
        ST = 128 * NQ
        NSEG = ST // 64

        def ln_stats_apply(src_view, dst_view, ready_by_q, stage, nq=NQ, extra_app_waits=()):
            app = []
            for q in range(nq):
                k.emit("dve", lambda e, q=q: e.bn_stats(bnst[:, 0, :], src_view[:, q, 0:512]),
                       waits=later(ready_by_q[min(q, len(ready_by_q) - 1)]), drain_before=True)
                k.emit("dve", lambda e, q=q: e.bn_stats(bnst[:, 1, :], src_view[:, q, 512:1024]))
                sm_ = k.emit("dve", lambda e, q=q, stage=stage: e.bn_aggr(
                    mvs[:, stage, q, :], bnst.ap().rearrange("p a b -> p (a b)")),
                    drain_before=True, inc=True)
                ln_m = k.emit("act", lambda e, q=q, stage=stage: e.activation(
                    out=rstds[:, stage, q:q + 1], in_=mvs[:, stage, q, 1:2], func=AF.Ln,
                    bias=epsc[:, 0:1], scale=1.0), waits=later(sm_, m_eps), inc=True)
                rs_m = k.emit("act", lambda e, q=q, stage=stage: e.activation(
                    out=rstds[:, stage, q:q + 1], in_=rstds[:, stage, q:q + 1],
                    func=AF.Exp, scale=-0.5), waits=later(ln_m), drain_before=True, inc=True)
                nm_ = k.emit("dve", lambda e, q=q, stage=stage: e.tensor_scalar(
                    out=nmr[:, stage, q:q + 1], in0=mvs[:, stage, q, 0:1],
                    scalar1=rstds[:, stage, q:q + 1], scalar2=-1.0,
                    op0=ALU.mult, op1=ALU.mult),
                    waits=later(rs_m), drain_before=True, inc=True)
                am = k.emit("gp", lambda e, q=q, stage=stage: e.tensor_scalar(
                    out=dst_view[:, q, :], in0=src_view[:, q, :],
                    scalar1=rstds[:, stage, q:q + 1], scalar2=nmr[:, stage, q:q + 1],
                    op0=ALU.mult, op1=ALU.add),
                    waits=later(nm_) + list(extra_app_waits), drain_before=True, inc=True)
                app.append(am)
            return app

        z1_mils = head_mils

        def transpose_tm_to_fm(src_tm, dst_fm, src_mils, nq=NQ, ret_tr=False):
            evac = []
            trs = []
            g = 0
            for q in range(nq):
                for ch in range(2):
                    b = g % 2
                    ws = later(src_mils[q]) + bank_write_waits(b) + const_ready
                    def tr(e, q=q, ch=ch, b=b):
                        last = None
                        for cc in range(4):
                            c = 4 * ch + cc
                            last = e.transpose(PSB3[:, b, 128 * cc:128 * cc + 128],
                                               src_tm[:, q, 128 * c:128 * c + 128], ids[:, :])
                        return last
                    tm_ = k.pe("tr", tr, waits=ws, inc=True)
                    bank_set_w(b, tm_)
                    trs.append(tm_)
                    def ev(e, q=q, ch=ch, b=b):
                        dst = dst_fm[:, 4 * ch:4 * ch + 4, 128 * q:128 * q + 128]
                        return e.activation(
                            out=dst, in_=PSB3[:, b, 0:512].rearrange("p (a c) -> p a c", c=128),
                            func=AF.Identity, bias=0.0, scale=1.0)
                    em_ = k.emit("act", ev, waits=later(tm_), inc=True)
                    bank_read(b, em_)
                    evac.append(em_)
                    g += 1
            if ret_tr:
                return evac, trs
            return evac

        z1fm_mils = transpose_tm_to_fm(z1_tm, z1_fm, z1_mils)
        z1fm_all = later(z1fm_mils[-1], *z1fm_mils)
        if pending_b4end is not None:
            pending_b4end()
            pending_b4end = None

        def qkcols(kc, c, which):
            off = 1024 * which + 128 * c
            return wqkv_sb[:, kc, off:off + 128]

        qk_mils = []
        gi = 0
        for c in range(8):
            for which in range(2):
                b = 2 + gi % 4
                ws = z1fm_all + bank_write_waits(b) + later(m_wqkv)
                def mmq(e, c=c, b=b, which=which, ST=ST):
                    last = None
                    for kc in range(8):
                        last = e.matmul(PS[:, b, 0:ST], qkcols(kc, c, which),
                                        z1_fm[:, kc, 0:ST], start=(kc == 0), stop=(kc == 7))
                    return last
                pm = k.pe("f", mmq, waits=ws, inc=True)
                bank_set_w(b, pm)
                dst = qf if which == 0 else kf
                if which == 0:
                    em = k.emit("act", lambda e, c=c, b=b, dst=dst, ST=ST: e.activation(
                        out=dst[:, c, :ST], in_=PS[:, b, 0:ST], func=AF.Identity,
                        bias=bqk8[:, c:c + 1], scale=0.125),
                        waits=later(pm, m_bqk8), inc=True)
                else:
                    em = k.emit("act", lambda e, c=c, b=b, dst=dst, ST=ST: e.activation(
                        out=dst[:, c, :ST], in_=PS[:, b, 0:ST], func=AF.Identity,
                        bias=bqk_sb[:, 8 + c:8 + c + 1], scale=1.0),
                        waits=later(pm), inc=True)
                bank_read(b, em)
                qk_mils.append(em)
                gi += 1
        qk_ready = later(*qk_mils)

        v_mils = []
        for q in range(NQ):
            for half in range(2):
                b = 2 + gi % 4
                ws = z1fm_all + bank_write_waits(b) + later(m_wqkv)
                def mmv(e, q=q, half=half, b=b):
                    last = None
                    for kc in range(8):
                        rhs = wqkv_sb[:, kc, 2048 + 512 * half:2048 + 512 * half + 512]
                        last = e.matmul(PS[:, b, :], z1_fm[:, kc, 128 * q:128 * q + 128], rhs,
                                        start=(kc == 0), stop=(kc == 7))
                    return last
                pm = k.pe("f", mmv, waits=ws, inc=True)
                bank_set_w(b, pm)
                em = k.emit("dve", lambda e, q=q, half=half, b=b: e.scalar_tensor_tensor(
                    out=vt[:, q, 512 * half:512 * half + 512], in0=PS[:, b, :],
                    scalar=1.0, in1=bvb[:, 512 * half:512 * half + 512],
                    op0=ALU.mult, op1=ALU.add),
                    waits=later(pm) + (later(prev_b4x3) if (q == 0 and half == 0) else []),
                    inc=True, drain_before=(q == 0 and half == 0))
                bank_read(b, em)
                v_mils.append(em)
                gi += 1
        v_ready = later(*v_mils)

        of_mils = []
        S_banks = [2, 3, 6, 7]
        s_mil = {}
        p_mil = {}

        def emit_S(c):
            b = S_banks[c % 4]
            ws = qk_ready + bank_write_waits(b)
            def mms(e, c=c, b=b, NSEG=NSEG):
                last = None
                for s in range(NSEG):
                    e.matmul(PS[0:64, b, 64 * s:64 * s + 64],
                             qf[0:64, c, 64 * s:64 * s + 64], kf[0:64, c, 64 * s:64 * s + 64],
                             start=True, stop=True, tile_position=(0, 0))
                    last = e.matmul(PS[64:128, b, 64 * s:64 * s + 64],
                                    qf[64:128, c, 64 * s:64 * s + 64], kf[64:128, c, 64 * s:64 * s + 64],
                                    start=True, stop=True, tile_position=(64, 64))
                return last
            pm = k.pe("q64", mms, waits=ws, inc=True)
            bank_set_w(b, pm)
            s_mil[c] = pm

        def emit_softmax(c):
            ci = c % 4
            b = S_banks[c % 4]
            ws = later(s_mil[c]) + later(exm_mil.get((c - 4) % 8))
            rm = k.emit("dve", lambda e, ci=ci, b=b, NSEG=NSEG, ST=ST: e.tensor_reduce(
                nmx[:, ci, 0:NSEG], PS[:, b, 0:ST].rearrange("p (s t) -> p s t", t=64),
                axis=AX.X, op=ALU.max, negate=True), waits=ws, drain_before=True, inc=True)
            exm = None
            psb_war = later(tr_mil.get((c - 4) % 8))
            for s in range(NSEG):
                exm = k.emit("act", lambda e, ci=ci, b=b, s=s: e.activation(
                    out=psb[:, ci, 64 * s:64 * s + 64], in_=PS[:, b, 64 * s:64 * s + 64],
                    func=AF.Exp, bias=nmx[:, ci, s:s + 1], scale=1.0),
                    waits=later(rm) + (psb_war if s == 0 else []),
                    inc=(True if s == NSEG - 1 else None))
            exm_mil[c] = exm
            bank_read(b, exm)
            sm_ = k.emit("dve", lambda e, ci=ci, NSEG=NSEG, ST=ST: e.tensor_reduce(
                ssum[:, ci, 0:NSEG], psb[:, ci, 0:ST].rearrange("p (s t) -> p s t", t=64),
                axis=AX.X, op=ALU.add), waits=later(exm), inc=True)
            rc = k.emit("dve", lambda e, ci=ci, NSEG=NSEG: e.reciprocal(
                rr[:, ci, 0:NSEG], ssum[:, ci, 0:NSEG]), waits=later(sm_),
                drain_before=True, inc=True)
            pm_ = None
            for s in range(NSEG):
                pm_ = k.emit("dve", lambda e, ci=ci, s=s: e.tensor_scalar(
                    out=psb[:, ci, 64 * s:64 * s + 64], in0=psb[:, ci, 64 * s:64 * s + 64],
                    scalar1=rr[:, ci, s:s + 1], scalar2=None, op0=ALU.mult, op1=ALU.bypass),
                    waits=later(rc), drain_before=(s == 0), inc=(True if s == NSEG - 1 else None))
            p_mil[c] = pm_

        def emit_T_pe(c):
            psrc = c % 4
            b = c % 2
            ws = later(p_mil[c]) + bank_write_waits(b)
            def trp(e, psrc=psrc, b=b, NQ=NQ):
                last = None
                for blk in range(NQ):
                    last = e.transpose(PSB3[:, b, 128 * blk:128 * blk + 128],
                                       psb[:, psrc, 128 * blk:128 * blk + 128], ids[:, :])
                return last
            tm_ = k.pe("tr", trp, waits=ws, inc=True)
            bank_set_w(b, tm_)
            tr_mil[c] = tm_

        def emit_T_evac(c):
            pci = c % 2
            b = c % 2
            ws2 = later(tr_mil[c]) + later(pv_mil.get((c - 2) % 8))
            em = k.emit("dve", lambda e, pci=pci, b=b, ST=ST: e.tensor_copy(
                pts[:, pci, 0:ST], PSB3[:, b, 0:ST]), waits=ws2, inc=True)
            bank_read(b, em)
            pt_evac[c] = em

        def mpv(e, c, ci, b, parity, NSEG):
            last = None
            for s in range(parity, NSEG, 2):
                sp = s % 2
                tq = s // 2
                for hp in range(2):
                    last = e.matmul(
                        PS[64 * hp:64 * hp + 64, b, 64 * s:64 * s + 64],
                        vt[64 * sp:64 * sp + 64, tq, 128 * c + 64 * hp:128 * c + 64 * hp + 64],
                        pts[64 * sp:64 * sp + 64, ci,
                            128 * (s // 2) + 64 * hp:128 * (s // 2) + 64 * hp + 64],
                        start=True, stop=True, tile_position=(64 * sp, 64 * hp))
            return last

        def emit_PV_parity(cs, parity):
            last = {}
            for c in cs:
                ci = c % 2
                b = 4 + ci
                ws = later(pt_evac[c]) + v_ready
                if parity == 0:
                    ws += bank_write_waits(b)
                last[c] = k.pe("pv%d" % parity,
                               lambda e, c=c, ci=ci, b=b, parity=parity, NSEG=NSEG: mpv(
                                   e, c, ci, b, parity, NSEG),
                               waits=ws, inc=(True if parity == 1 else None))
            return last

        def emit_O_evac(c):
            b = 4 + c % 2
            pvm = pv_mil[c]
            bank_set_w(b, pvm)
            em = k.emit("act", lambda e, c=c, b=b, ST=ST: e.activation(
                out=of[:, c, 0:ST], in_=PS[:, b, 0:ST], func=AF.Identity, bias=0.0, scale=1.0),
                waits=later(pvm), inc=True)
            bank_read(b, em)
            of_mils.append(em)

        emit_S(0)
        emit_softmax(0)
        emit_S(1)
        emit_softmax(1)
        emit_S(2)
        emit_S(3)
        for pr in range(4):
            c0, c1 = 2 * pr, 2 * pr + 1
            emit_T_pe(c0)
            emit_T_pe(c1)
            emit_T_evac(c0)
            emit_T_evac(c1)
            for cn in (c0 + 2, c1 + 2):
                if cn < 8 and cn >= 4:
                    pass
            if c0 + 2 < 8 and c0 + 2 >= 2:
                if c0 + 2 >= 2 and (c0 + 2) not in p_mil:
                    emit_softmax(c0 + 2)
                if c1 + 2 < 8 and (c1 + 2) not in p_mil:
                    emit_softmax(c1 + 2)
            emit_PV_parity((c0, c1), 0)
            pv2 = emit_PV_parity((c0, c1), 1)
            pv_mil[c0] = pv2[c0]
            pv_mil[c1] = pv2[c1]
            emit_O_evac(c0)
            emit_O_evac(c1)
            for cn in (2 * pr + 4, 2 * pr + 5):
                if cn < 8:
                    emit_S(cn)
        of_ready = later(*of_mils)

        x2_mils = []
        prev_resadd = None
        for q in range(NQ):
            bpair = (6, 7) if q % 2 == 0 else (2, 3)
            ap_mils = []
            for jh in range(2):
                b = bpair[jh]
                ws = of_ready + bank_write_waits(b) + later(m_wo)
                def mmo(e, q=q, jh=jh, b=b):
                    last = None
                    for c in range(8):
                        last = e.matmul(PS[:, b, :], of[:, c, 128 * q:128 * q + 128],
                                        wo_sb[:, c, 512 * jh:512 * jh + 512],
                                        start=(c == 0), stop=(c == 7))
                    return last
                pm = k.pe("f", mmo, waits=ws, inc=True)
                bank_set_w(b, pm)
                em = k.emit("dve", lambda e, jh=jh, b=b: e.scalar_tensor_tensor(
                    out=aps[:, 0, 512 * jh:512 * jh + 512], in0=PS[:, b, :], scalar=1.0,
                    in1=bob[:, 512 * jh:512 * jh + 512], op0=ALU.mult, op1=ALU.add),
                    waits=later(pm), drain_before=(jh == 0), inc=True)
                bank_read(b, em)
                ap_mils.append(em)
            k.emit("dve", lambda e: e.bn_stats(bnst[:, 0, :], aps[:, 0, 0:512]),
                   waits=later(*ap_mils), drain_before=True)
            k.emit("dve", lambda e: e.bn_stats(bnst[:, 1, :], aps[:, 0, 512:1024]))
            ag = k.emit("dve", lambda e, q=q: e.bn_aggr(
                mvs[:, 1, q, :], bnst.ap().rearrange("p a b -> p (a b)")), drain_before=True, inc=True)
            ln_m = k.emit("act", lambda e, q=q: e.activation(
                out=rstds[:, 1, q:q + 1], in_=mvs[:, 1, q, 1:2], func=AF.Ln, bias=epsc[:, 0:1], scale=1.0),
                waits=later(ag, m_eps), inc=True)
            rs_m = k.emit("act", lambda e, q=q: e.activation(
                out=rstds[:, 1, q:q + 1], in_=rstds[:, 1, q:q + 1], func=AF.Exp, scale=-0.5),
                waits=later(ln_m), drain_before=True, inc=True)
            ts_m = k.emit("dve", lambda e, q=q: e.tensor_scalar(
                out=z2t[:, 0, :], in0=aps[:, 0, :], scalar1=mvs[:, 1, q, 0:1],
                scalar2=rstds[:, 1, q:q + 1], op0=ALU.subtract, op1=ALU.mult),
                waits=later(rs_m, prev_resadd), drain_before=True, inc=True)
            x2m = k.emit("dve", lambda e, q=q: e.tensor_tensor(
                out=x_tm[:, q, :], in0=xa[:, q, :], in1=z2t[:, 0, :], op=ALU.add),
                waits=later(ts_m) + [("do", k.ct["do"])], drain_before=True, inc=True)
            prev_resadd = x2m
            x2_mils.append(x2m)

        if attn_only:
            od = k.emit("sync", lambda e, T0=T0, NQ=NQ: e.dma_start(
                out=outh[T0:T0 + 128 * NQ, :].rearrange("(q p) d -> p q d", p=128),
                in_=x_tm[:, 0:NQ, :]), waits=later(*x2_mils), inc=("do", 16))
            prev_x2_mils = x2_mils
            prev_ytr = []
            if sub_i + 1 < len(subs):
                head_mils = emit_head(sub_i + 1)
            continue

        y_mils = ln_stats_apply(x_tm, z1_tm, x2_mils, 2)
        yfm_mils, ytr_mils = transpose_tm_to_fm(z1_tm, z1_fm, y_mils, ret_tr=True)
        yfm_all = later(yfm_mils[-1], *yfm_mils)

        sig_mils = []
        for q in range(NQ):
            b = 6 + q % 2
            ws = yfm_all + bank_write_waits(b) + later(m_gw)
            def mmg(e, q=q, b=b):
                last = None
                for kc in range(8):
                    last = e.matmul(PS[:, b, 0:NE], z1_fm[:, kc, 128 * q:128 * q + 128],
                                    gatew_sb[:, kc, :], start=(kc == 0), stop=(kc == 7))
                return last
            pm = k.pe("f", mmg, waits=ws, inc=True)
            bank_set_w(b, pm)
            em = k.emit("dve", lambda e, q=q, b=b: e.scalar_tensor_tensor(
                out=gl[:, q, :], in0=PS[:, b, 0:NE], scalar=1.0, in1=gbb[:, :],
                op0=ALU.mult, op1=ALU.add), waits=later(pm), inc=True)
            bank_read(b, em)
            ex = k.emit("act", lambda e, q=q: e.activation(
                out=ge[:, q, :], in_=gl[:, q, :], func=AF.Exp, scale=-1.0),
                waits=later(em), inc=True)
            t1 = k.emit("dve", lambda e, q=q: e.tensor_scalar(
                out=ge[:, q, :], in0=ge[:, q, :], scalar1=1.0, scalar2=None,
                op0=ALU.add, op1=ALU.bypass), waits=later(ex), inc=True)
            sg = k.emit("dve", lambda e, q=q: e.reciprocal(gl[:, q, :], ge[:, q, :]),
                        waits=later(t1), drain_before=True, inc=True)
            sig_mils.append(sg)

        wsb_mils = []
        for q in range(NQ):
            k.emit("dve", lambda e, q=q: e.tensor_reduce(
                m1[:, q:q + 1], gl[:, q, :], axis=AX.X, op=ALU.max),
                waits=later(sig_mils[q]), drain_before=True)
            k.emit("dve", lambda e, q=q: e.tensor_scalar(
                out=eqm[:, q, :], in0=gl[:, q, :], scalar1=m1[:, q:q + 1], scalar2=-1e30,
                op0=ALU.is_ge, op1=ALU.mult), drain_before=True)
            k.emit("dve", lambda e, q=q: e.tensor_tensor(
                out=eqm[:, q, :], in0=gl[:, q, :], in1=eqm[:, q, :], op=ALU.add), drain_before=True)
            k.emit("dve", lambda e, q=q: e.tensor_reduce(
                m2[:, q:q + 1], eqm[:, q, :], axis=AX.X, op=ALU.max), drain_before=True)
            k.emit("dve", lambda e, q=q: e.tensor_tensor(
                out=den[:, q:q + 1], in0=m1[:, q:q + 1], in1=m2[:, q:q + 1], op=ALU.add),
                drain_before=True)
            k.emit("dve", lambda e, q=q: e.reciprocal(den[:, q:q + 1], den[:, q:q + 1]),
                   drain_before=True)
            k.emit("dve", lambda e, q=q: e.tensor_scalar(
                out=eqm[:, q, :], in0=gl[:, q, :], scalar1=m2[:, q:q + 1], scalar2=None,
                op0=ALU.is_ge, op1=ALU.bypass), drain_before=True)
            k.emit("dve", lambda e, q=q: e.tensor_tensor(
                out=w_tm[:, q, :], in0=gl[:, q, :], in1=eqm[:, q, :], op=ALU.mult),
                drain_before=True)
            k.emit("dve", lambda e, q=q: e.tensor_scalar(
                out=w_tm[:, q, :], in0=w_tm[:, q, :], scalar1=den[:, q:q + 1], scalar2=None,
                op0=ALU.mult, op1=ALU.bypass), drain_before=True)
            wm = k.emit("dve", lambda e, q=q: e.tensor_copy(wsb[:, q, :], w_tm[:, q, :]),
                        drain_before=True, inc=True)
            wsb_mils.append(wm)
        h_prev = None
        h_mil = [None] * NEXP
        down_mil = [None] * NEXP
        up_last = [None] * NEXP
        wd_mil = [None] * NEXP
        last_down = {}
        acc_banks = {}
        acc_first = {}
        acc_war = {}

        def emit_ups(eidx):
            nonlocal h_prev
            wb_i = eidx % 2
            wm = wgu_loaded[wb_i]
            ev_g = []
            for i in range(4):
                b = i % 2
                ws = yfm_all + bank_write_waits(b) + later(wm)
                def mmug(e, i=i, b=b, wb_i=wb_i, ST=ST):
                    last = None
                    for kc in range(8):
                        last = e.matmul(PS[:, b, 0:ST], wg_sb[:, wb_i, kc, 128 * i:128 * i + 128],
                                        z1_fm[:, kc, 0:ST], start=(kc == 0), stop=(kc == 7))
                    return last
                pm = k.pe("f", mmug, waits=ws, inc=True)
                bank_set_w(b, pm)
                em = k.emit("act", lambda e, i=i, b=b, hp_=eidx % 2, ST=ST: e.activation(
                    out=gs[:, hp_, i, 0:ST], in_=PS[:, b, 0:ST], func=AF.Silu, bias=0.0, scale=1.0),
                    waits=later(pm, h_mil[eidx - 2] if eidx >= 2 else None), inc=True)
                bank_read(b, em)
                ev_g.append(em)
            ev_u = []
            for i in range(4):
                b = i % 2
                ws = yfm_all + bank_write_waits(b) + later(wm)
                def mmuu(e, i=i, b=b, wb_i=wb_i, ST=ST):
                    last = None
                    for kc in range(8):
                        last = e.matmul(PS[:, b, 0:ST], wu_sb[:, wb_i, kc, 128 * i:128 * i + 128],
                                        z1_fm[:, kc, 0:ST], start=(kc == 0), stop=(kc == 7))
                    return last
                pm = k.pe("f", mmuu, waits=ws, inc=True)
                bank_set_w(b, pm)
                up_last[eidx] = pm
                war_u = h_mil[eidx - 2] if eidx >= 2 else None
                if eidx == 0:
                    em = k.emit("dve", lambda e, i=i, b=b, hp_=eidx % 2, ST=ST: e.tensor_copy(
                        us[:, hp_, i, 0:ST], PS[:, b, 0:ST]), waits=later(pm, war_u), inc=True)
                else:
                    em = k.emit("dve", lambda e, i=i, b=b, hp_=eidx % 2, eidx=eidx, ST=ST: e.tensor_tensor(
                        out=us[:, hp_, i, 0:ST], in0=PS[:, b, 0:ST], in1=wbc[:, eidx - 1, 0:ST],
                        op=ALU.mult),
                        waits=later(pm, war_u, wbc_mils[eidx - 1]), inc=True)
                bank_read(b, em)
                ev_u.append(em)
            h_m = None
            hp_ = eidx % 2
            war = later(down_mil[eidx - 2]) if eidx >= 2 else []
            for i in range(4):
                h_m = k.emit("dve", lambda e, i=i, hp_=hp_, ST=ST: e.tensor_tensor(
                    out=hs[:, hp_, i, 0:ST], in0=gs[:, hp_, i, 0:ST], in1=us[:, hp_, i, 0:ST],
                    op=ALU.mult),
                    waits=later(ev_g[i], ev_u[i]) + (war if i == 0 else []),
                    drain_before=(i == 0), inc=(True if i == 3 else None))
            h_mil[eidx] = h_m
            h_prev = h_m

        def emit_downs(eidx):
            wb_i = eidx % 2
            hp_ = eidx % 2
            dm = None
            wdm = wd_loaded[wb_i]
            for tq in range(NQ):
                for jh in range(2):
                    bb = acc_banks[(tq, jh)]
                    ws = later(h_mil[eidx], wdm) + (acc_war[bb] if acc_first[bb] else [])
                    def mmd(e, tq=tq, jh=jh, bb=bb, wb_i=wb_i, hp_=hp_,
                            first=acc_first[bb], eidx=eidx):
                        last = None
                        for i in range(4):
                            last = e.matmul(PS[:, bb, :], hs[:, hp_, i, 128 * tq:128 * tq + 128],
                                            wd_sb[:, wb_i, i, 512 * jh:512 * jh + 512],
                                            start=(first and i == 0),
                                            stop=(eidx == NEXP - 1 and i == 3))
                        return last
                    dm = k.pe("f", mmd, waits=ws,
                              inc=(True if (tq == NQ - 1 and jh == 1) or eidx == NEXP - 1 else None))
                    if eidx == NEXP - 1:
                        bank_set_w(bb, dm)
                        last_down[(tq, jh)] = dm
                    acc_first[bb] = False
            down_mil[eidx] = dm

        prev_x2_mils = x2_mils
        prev_ytr = ytr_mils
        wbc_mils = []
        for eidx in range(NEXP):
            emit_ups(eidx)
            if eidx == 0:
                b = 0
                ws = later(*wsb_mils) + bank_write_waits(b)
                def trw(e, b=b, NQ=NQ):
                    last = None
                    for q in range(NQ):
                        last = e.transpose(PSB3[0:16, b, 128 * q:128 * q + 128],
                                           wsb[:, q, :], ids[:, :])
                    return last
                tw = k.pe("tr", trw, waits=ws, inc=True)
                bank_set_w(b, tw)
                ew = k.emit("dve", lambda e, b=b, ST=ST: e.tensor_copy(
                    wfm[0:16, 0:ST], PSB3[0:16, b, 0:ST]), waits=later(tw), inc=True)
                bank_read(b, ew)
                del wbc_mils[:]
                slabA_free = later(*[m for m in of_mils])
                for ei2 in range(NE):
                    b = 6 + ei2 % 2
                    ws = later(ew) + bank_write_waits(b) + (slabA_free if ei2 < 2 else [])
                    pm = k.pe("k32", lambda e, ei2=ei2, b=b, ST=ST: e.matmul(
                        PS[:, b, 0:ST], eall[0:16, 128 * ei2:128 * ei2 + 128], wfm[0:16, 0:ST],
                        start=True, stop=True), waits=ws, inc=True)
                    bank_set_w(b, pm)
                    em = k.emit("act", lambda e, ei2=ei2, b=b, ST=ST: e.activation(
                        out=wbc[:, ei2, 0:ST], in_=PS[:, b, 0:ST], func=AF.Identity,
                        bias=0.0, scale=1.0), waits=later(pm), inc=True)
                    bank_read(b, em)
                    wbc_mils.append(em)
                wbc_ready = later(*wbc_mils)
                acc_banks.clear(); acc_first.clear(); acc_war.clear()
                for tq in range(NQ):
                    for jh in range(2):
                        bb = 2 + 2 * tq + jh
                        acc_banks[(tq, jh)] = bb
                for bb in set(acc_banks.values()):
                    acc_first[bb] = True
                    acc_war[bb] = bank_write_waits(bb)
            if eidx + 2 < NEXP:
                wgu_loaded[eidx % 2] = load_gu(eidx + 2, eidx % 2, later(up_last[eidx]))
            if eidx > 0:
                emit_downs(eidx - 1)
                if eidx + 1 < NEXP:
                    wd_loaded[(eidx + 1) % 2] = load_d(eidx + 1, (eidx + 1) % 2,
                                                       later(down_mil[eidx - 1]))
            if sub_i + 1 < len(subs):
                if eidx == 8:
                    head_xin = emit_head_x(sub_i + 1)
                    head_mils = []
                elif eidx in (10, 12, 14):
                    q = (eidx - 10) // 2
                    if q < subs[sub_i + 1][1]:
                        head_mils.append(emit_head_q(head_xin, q))
        emit_downs(NEXP - 1)
        if sub_i + 1 < len(subs):
            wgu_loaded[0] = load_gu(0, 0, later(up_last[NEXP - 1]))
            wgu_loaded[1] = load_gu(1, 1, later(up_last[NEXP - 2]))
            wd_loaded[1] = load_d(1, 1, later(down_mil[NEXP - 2]))
            wd_loaded[0] = load_d(0, 0, later(down_mil[NEXP - 1]))
        if len(subs) == 1 or (sub_i + 1 == len(subs)):
            pass

        def b4_end(NQ=NQ, T0=T0, acc_banks=dict(acc_banks), last_down=dict(last_down),
                   prev_resadd=prev_resadd):
            nonlocal prev_b4x3
            ffn_done = []
            prev_x3 = prev_resadd
            for tq in range(NQ):
                yrc = k.emit("dve", lambda e, tq=tq: e.tensor_scalar(
                    out=z2t[:, 0, :], in0=x_tm[:, tq, :], scalar1=mvs[:, 2, tq, 0:1],
                    scalar2=rstds[:, 2, tq:tq + 1], op0=ALU.subtract, op1=ALU.mult),
                    waits=later(last_down[(tq, 0)], prev_x3), drain_before=True, inc=True)
                apm = []
                for jh in range(2):
                    bb = acc_banks[(tq, jh)]
                    em = k.emit("dve", lambda e, tq=tq, jh=jh, bb=bb: e.scalar_tensor_tensor(
                        out=aps[:, 0, 512 * jh:512 * jh + 512], in0=PS[:, bb, :], scalar=1.0,
                        in1=z2t[:, 0, 512 * jh:512 * jh + 512], op0=ALU.mult, op1=ALU.add),
                        waits=later(last_down[(tq, jh)], yrc), drain_before=True, inc=True)
                    bank_read(bb, em)
                    apm.append(em)
                k.emit("dve", lambda e: e.bn_stats(bnst[:, 0, :], aps[:, 0, 0:512]),
                       waits=later(*apm), drain_before=True)
                k.emit("dve", lambda e: e.bn_stats(bnst[:, 1, :], aps[:, 0, 512:1024]))
                ag = k.emit("dve", lambda e, tq=tq: e.bn_aggr(
                    mvs[:, 3, tq, :], bnst.ap().rearrange("p a b -> p (a b)")), drain_before=True, inc=True)
                ln_m = k.emit("act", lambda e, tq=tq: e.activation(
                    out=rstds[:, 3, tq:tq + 1], in_=mvs[:, 3, tq, 1:2], func=AF.Ln, bias=epsc[:, 0:1], scale=1.0),
                    waits=later(ag, m_eps), inc=True)
                rs_m = k.emit("act", lambda e, tq=tq: e.activation(
                    out=rstds[:, 3, tq:tq + 1], in_=rstds[:, 3, tq:tq + 1], func=AF.Exp, scale=-0.5),
                    waits=later(ln_m), drain_before=True, inc=True)
                ts_m = k.emit("dve", lambda e, tq=tq: e.tensor_scalar(
                    out=z2t[:, 0, :], in0=aps[:, 0, :], scalar1=mvs[:, 3, tq, 0:1],
                    scalar2=rstds[:, 3, tq:tq + 1], op0=ALU.subtract, op1=ALU.mult),
                    waits=later(rs_m), drain_before=True, inc=True)
                x3m = k.emit("dve", lambda e, tq=tq: e.tensor_tensor(
                    out=x_tm[:, tq, :], in0=x_tm[:, tq, :], in1=z2t[:, 0, :], op=ALU.add),
                    waits=later(ts_m), drain_before=True, inc=True)
                prev_x3 = x3m
                prev_b4x3 = x3m
                ffn_done.append(x3m)
            k.emit("sync", lambda e, T0=T0, NQ=NQ: e.dma_start(
                out=outh[T0:T0 + 128 * NQ, :].rearrange("(q p) d -> p q d", p=128),
                in_=x_tm[:, 0:NQ, :]), waits=later(*ffn_done), inc=("do", 16))
        pending_b4end = b4_end
        if sub_i + 1 == len(subs):
            pending_b4end()
            pending_b4end = None

    k.emit("sync", lambda e: None, waits=[(s, k.ct[s]) for s in
                                          ("do", "dx", "dwg0", "dwg1", "dwd0", "dwd1", "dc", "dcg", "dcv0", "dcv1", "dcd0", "dcd1", "dss")])
    k.replay(ctx)
    ctx.close()
    return nc


from concourse.bass_utils import run_bass_kernel_spmd


def _permute_wqkv(w_qkv):
    w = np.asarray(w_qkv, np.float32)
    idx = np.zeros(3 * D, np.int64)
    for c in range(8):
        for j in range(128):
            h = 2 * c + (1 if j >= 64 else 0)
            m = j % 64
            idx[128 * c + j] = h * 192 + m
            idx[1024 + 128 * c + j] = h * 192 + 64 + m
    for h in range(16):
        for m in range(64):
            idx[2048 + 64 * h + m] = h * 192 + 128 + m
    return np.ascontiguousarray(w[:, idx])


def _host_layout(b_qkv, b_o, gate_b):
    bqk = np.zeros((128, 16), np.float32)
    b_qkv = np.asarray(b_qkv, np.float32)
    for c in range(8):
        for p in range(128):
            h = 2 * c + (1 if p >= 64 else 0)
            m = p % 64
            bqk[p, c] = b_qkv[h * 192 + m]
            bqk[p, 8 + c] = b_qkv[h * 192 + 64 + m]
    bvrow = np.zeros((1, D), np.float32)
    for h in range(16):
        bvrow[0, 64 * h:64 * h + 64] = b_qkv[h * 192 + 128:h * 192 + 192]
    eallv = np.zeros((16, 2048), np.float32)
    for e in range(16):
        eallv[e, 128 * e:128 * (e + 1)] = 1.0
    return dict(
        bqk=bqk, bvrow=bvrow,
        borow=np.asarray(b_o, np.float32).reshape(1, D),
        gbrow=np.asarray(gate_b, np.float32).reshape(1, NE),
        identh=np.eye(128).astype(NPHALF),
        eallh=eallv.astype(NPHALF),
    )


def kernel(L0_attn_x, L0_attn_w_qkv, L0_attn_b_qkv, L0_attn_w_o, L0_attn_b_o,
           L0_ffn_gate_w, L0_ffn_gate_b, L0_ffn_shared_wg, L0_ffn_shared_wu,
           L0_ffn_shared_wd, L0_ffn_experts_wg, L0_ffn_experts_wu, L0_ffn_experts_wd):
    x = np.ascontiguousarray(np.asarray(L0_attn_x, np.float32))
    B, S, Dm = x.shape
    assert (B, S, Dm) == (8, 4096, 1024)
    consts = _host_layout(L0_attn_b_qkv, L0_attn_b_o, L0_ffn_gate_b)
    shared = dict(
        wqkv=_permute_wqkv(L0_attn_w_qkv),
        wo=np.ascontiguousarray(np.asarray(L0_attn_w_o, np.float32)),
        gatew=np.ascontiguousarray(np.asarray(L0_ffn_gate_w, np.float32)),
        swg=np.ascontiguousarray(np.asarray(L0_ffn_shared_wg, np.float32)),
        swu=np.ascontiguousarray(np.asarray(L0_ffn_shared_wu, np.float32)),
        swd=np.ascontiguousarray(np.asarray(L0_ffn_shared_wd, np.float32)),
        ewg=np.ascontiguousarray(np.asarray(L0_ffn_experts_wg, np.float32)),
        ewu=np.ascontiguousarray(np.asarray(L0_ffn_experts_wu, np.float32)),
        ewd=np.ascontiguousarray(np.asarray(L0_ffn_experts_wd, np.float32)),
        **consts)
    nc = build(S)
    in_maps = [dict(x=x[c], **shared) for c in range(8)]
    res = run_bass_kernel_spmd(nc, in_maps, core_ids=list(range(8)))
    return np.stack([r["out"] for r in res.results], axis=0)


# version 24
# speedup vs baseline: 1.0211x; 1.0117x over previous
import numpy as np
import ml_dtypes
import concourse.bass as bass
import concourse.mybir as mybir

F32 = mybir.dt.float32
BF16 = mybir.dt.bfloat16
FP16 = mybir.dt.float16
HALF = FP16
NPHALF = ml_dtypes.float16 if hasattr(ml_dtypes, 'float16') else None
import numpy as _np
NPHALF = _np.float16
U8 = mybir.dt.uint8
AF = mybir.ActivationFunctionType
ALU = mybir.AluOpType
AX = mybir.AxisListType

D = 1024
NE = 16
NEXP = 17
DI = 512


class Sch:
    ENGS = ("pe", "act", "dve", "gp", "sync")

    def __init__(self, nc):
        self.nc = nc
        self.prog = {e: [] for e in self.ENGS}
        self.semnames = ["pe", "act", "dve", "gp_c", "dx", "do", "dwg0", "dwg1", "dwd0", "dwd1", "dc", "dcg", "dcv0", "dcv1", "dcd0", "dcd1", "dss"]
        self.ct = {s: 0 for s in self.semnames}
        self.pe_mode = None

    def eng_sem(self, eng):
        return {"pe": "pe", "act": "act", "dve": "dve", "gp": "gp_c"}.get(eng)

    def emit(self, eng, fn, waits=(), inc=None, drain_before=False):
        DMAS = ("dx", "do", "dwg0", "dwg1", "dwd0", "dwd1", "dc", "dcg", "dcv0", "dcv1", "dcd0", "dcd1", "dss")
        wl = {}
        own = self.eng_sem(eng)
        for w in waits:
            if w is None:
                continue
            s, v = w
            if s == own:
                if eng != "pe":
                    drain_before = True
                continue
            if not (self.ct[s] >= v >= 0):
                raise AssertionError(f"wait on future milestone {s}>={v} (now {self.ct[s]})")
            if s in DMAS:
                v = self.ct[s]
            wl[s] = max(wl.get(s, 0), v)
        mil = None
        if inc is True:
            s = own
            self.ct[s] += 1
            mil = (s, self.ct[s])
        elif inc:
            s, amt = inc
            self.ct[s] += amt
            mil = (s, self.ct[s])
        self.prog[eng].append(dict(waits=sorted(wl.items()), fn=fn, inc=mil,
                                   drain_before=drain_before))
        return mil

    PE_DRAIN_MODES = ("pv0", "pv1", "k32")

    def pe_filler(self, fn, n):
        self.prog["pe"].append(dict(waits=[], fn=lambda e, fn=fn, n=n: fn(e, n), inc=None,
                                    drain_before=False))

    def pe(self, mode, fn, waits=(), inc=None):
        db = (mode != self.pe_mode) and (
            mode in self.PE_DRAIN_MODES or self.pe_mode in self.PE_DRAIN_MODES)
        self.pe_mode = mode
        return self.emit("pe", fn, waits=waits, inc=inc, drain_before=db)

    def replay(self, ctx):
        nc = self.nc
        sems = {s: ctx.enter_context(nc.semaphore(f"sem_{s}")) for s in self.semnames}
        with nc.Block() as block:
            def body_for(ename):
                def body(eng):
                    for st_ in self.prog[ename]:
                        for s, v in st_["waits"]:
                            eng.wait_ge(sems[s], v)
                        if st_["drain_before"]:
                            eng.drain()
                        ret = st_["fn"](eng)
                        if st_["inc"] is not None:
                            s, v = st_["inc"]
                            amt = 16 if s != "pe" and s != "act" and s != "dve" and s != "gp_c" else 1
                            if ret is not None:
                                ret.then_inc(sems[s], amt)
                            else:
                                eng.sem_inc(sems[s], amt)
                return body
            block.sync(body_for("sync"))
            block.gpsimd(body_for("gp"))
            block.tensor(body_for("pe"))
            block.scalar(body_for("act"))
            block.vector(body_for("dve"))


def later(*mils):
    return [m for m in mils if m is not None]


def build(ntok, attn_only=False, skip_scratch=False):
    nc = bass.Bass("TRN2", target_bir_lowering=False)
    subs = []
    t = 0
    while ntok - t >= 384:
        if ntok - t == 512:
            subs.append((t, 2)); t += 256
            subs.append((t, 2)); t += 256
            break
        subs.append((t, 3)); t += 384
    if ntok - t == 256:
        subs.append((t, 2)); t += 256
    elif ntok - t == 128:
        subs.append((t, 1)); t += 128
    assert t == ntok, (t, ntok)

    xh = nc.dram_tensor("x", [ntok, D], F32, kind="ExternalInput")
    wqkv = nc.dram_tensor("wqkv", [D, 3 * D], F32, kind="ExternalInput")
    wo = nc.dram_tensor("wo", [D, D], F32, kind="ExternalInput")
    gatew = nc.dram_tensor("gatew", [D, NE], F32, kind="ExternalInput")
    swg = nc.dram_tensor("swg", [D, DI], F32, kind="ExternalInput")
    swu = nc.dram_tensor("swu", [D, DI], F32, kind="ExternalInput")
    swd = nc.dram_tensor("swd", [DI, D], F32, kind="ExternalInput")
    ewg = nc.dram_tensor("ewg", [NE, D, DI], F32, kind="ExternalInput")
    ewu = nc.dram_tensor("ewu", [NE, D, DI], F32, kind="ExternalInput")
    ewd = nc.dram_tensor("ewd", [NE, DI, D], F32, kind="ExternalInput")
    bqk = nc.dram_tensor("bqk", [128, 16], F32, kind="ExternalInput")
    bvrow = nc.dram_tensor("bvrow", [1, D], F32, kind="ExternalInput")
    borow = nc.dram_tensor("borow", [1, D], F32, kind="ExternalInput")
    gbrow = nc.dram_tensor("gbrow", [1, NE], F32, kind="ExternalInput")
    identh = nc.dram_tensor("identh", [128, 128], HALF, kind="ExternalInput")
    eallh = nc.dram_tensor("eallh", [16, 16 * 128], HALF, kind="ExternalInput")
    outh = nc.dram_tensor("out", [ntok, D], F32, kind="ExternalOutput")
    scrg = nc.dram_tensor("scrg", [NEXP, 128, 8 * DI], HALF)
    scru = nc.dram_tensor("scru", [NEXP, 128, 8 * DI], HALF)
    scrd = nc.dram_tensor("scrd", [NEXP, 128, 4 * D], HALF)

    from contextlib import ExitStack
    ctx = ExitStack()
    sb = lambda name, shape, dt: ctx.enter_context(nc.sbuf_tensor(name, shape, dt))

    MQ = 3
    MT = 128 * MQ
    x_tm = sb("x_tm", [128, MQ, D], F32)
    xa = sb("xa", [128, MQ, D], F32)
    z1_tm = sb("z1_tm", [128, MQ, D], HALF)
    z1_fm = sb("z1_fm", [128, 8, MT], HALF)
    psb = sb("psb", [128, 4, MT], HALF)
    pts = sb("pts", [128, 2, MT], HALF)
    wqkv_sb = sb("wqkv_sb", [128, 8, 3 * D], HALF)
    wo_sb = sb("wo_sb", [128, 8, D], HALF)
    gatew_sb = sb("gatew_sb", [128, 8, NE], HALF)
    wg_sb = sb("wg_sb", [128, 2, 8, DI], HALF)
    wu_sb = sb("wu_sb", [128, 2, 8, DI], HALF)
    wd_sb = sb("wd_sb", [128, 2, 4, D], HALF)
    ids = sb("ids", [128, 128], HALF)
    eall = sb("eall", [16, 16 * 128], HALF)
    bqk_sb = sb("bqk_sb", [128, 16], F32)
    bqk8 = sb("bqk8", [128, 8], F32)
    bvb = sb("bvb", [128, D], HALF)
    bob = sb("bob", [128, D], HALF)
    gbb = sb("gbb", [128, NE], F32)
    epsc = sb("epsc", [128, 1], F32)
    bnst = sb("bnst", [128, 2, 6], F32)
    mvs = sb("mvs", [128, 4, MQ, 2], F32)
    rstds = sb("rstds", [128, 4, MQ], F32)
    nmr = sb("nmr", [128, 4, MQ], F32)
    nmx = sb("nmx", [128, 4, 6], F32)
    ssum = sb("ssum", [128, 4, 6], F32)
    rr = sb("rr", [128, 4, 6], F32)
    gl = sb("gl", [128, MQ, NE], F32)
    ge = sb("ge", [128, MQ, NE], F32)
    m1 = sb("m1", [128, MQ], F32)
    m2 = sb("m2", [128, MQ], F32)
    den = sb("den", [128, MQ], F32)
    eqm = sb("eqm", [128, MQ, NE], F32)
    w_tm = sb("w_tm", [128, MQ, NE], F32)
    wsb = sb("wsb", [128, MQ, NE], HALF)
    wfm = sb("wfm", [16, MT], HALF)
    gs = sb("gs", [128, 2, 4, MT], HALF)
    us = sb("us", [128, 2, 4, MT], HALF)
    hs = sb("hs", [128, 2, 4, MT], HALF)
    slabA = sb("slabA", [128, 12288], U8)
    slabC = sb("slabC", [128, 8192], U8)
    of = sb("of", [128, 8, MT], HALF)

    def view(slab, off_bytes, dt, shape):
        ap = slab.ap().bitcast(dt)
        esz = mybir.dt.size(dt)
        n = 1
        for s in shape[1:]:
            n *= s
        v = ap[:, off_bytes // esz: off_bytes // esz + n]
        if len(shape) == 3:
            v = v.rearrange("p (a b) -> p a b", b=shape[2])
        return v

    qf = view(slabA, 0, HALF, [128, 8, MT])
    kf = view(slabA, 6144, HALF, [128, 8, MT])
    wbc = view(slabA, 0, FP16, [128, NE, MT])
    vt = view(slabC, 0, HALF, [128, MQ, D])
    aps = view(slabC, 0, F32, [128, 1, D])
    z2t = view(slabC, 4096, F32, [128, 1, D])

    PSUMT = ctx.enter_context(nc.psum_tensor("PSA", [128, 8, 512], F32))
    PS = PSUMT.ap()
    PSB3 = PSUMT.ap().bitcast(HALF)

    k = Sch(nc)
    bank_w = [None] * 8
    bank_r = [[] for _ in range(8)]

    def bank_write_waits(b):
        ws = list(bank_r[b]) + later(bank_w[b])
        bank_r[b] = []
        return ws

    def bank_set_w(b, mil):
        bank_w[b] = mil

    def bank_read(b, mil):
        bank_r[b].append(mil)

    m_eps = k.emit("gp", lambda e: e.memset(epsc[:, :], 1e-5), inc=True)
    cst = [
        k.emit("sync", lambda e: e.dma_start(out=ids[:, :], in_=identh[:, :]), inc=("dc", 16)),
        k.emit("sync", lambda e: e.dma_start(out=eall[:, :], in_=eallh[:, :]), inc=("dc", 16)),
        k.emit("sync", lambda e: e.dma_start(out=bqk_sb[:, :], in_=bqk[:, :]), inc=("dc", 16)),
        k.emit("gp", lambda e: e.dma_start(out=bvb[:, :], in_=bvrow.ap().broadcast_to([128, D])), inc=("dcg", 16)),
        k.emit("gp", lambda e: e.dma_start(out=bob[:, :], in_=borow.ap().broadcast_to([128, D])), inc=("dcg", 16)),
        k.emit("sync", lambda e: e.dma_start(out=gbb[:, :], in_=gbrow.ap().broadcast_to([128, NE])), inc=("dc", 16)),
    ]
    m_bqk8 = k.emit("dve", lambda e: e.tensor_scalar(
        out=bqk8[:, :], in0=bqk_sb[:, 0:8], scalar1=0.125, scalar2=None,
        op0=ALU.mult, op1=ALU.bypass), waits=later(cst[2]), inc=True)
    wq_r = wqkv.rearrange("(k p) f -> p k f", p=128)
    m_wqkv = []
    for h in range(4):
        m_wqkv.append(k.emit("gp", lambda e, h=h: e.dma_start(
            out=wqkv_sb[:, 2 * h:2 * h + 2, :], in_=wq_r[:, 2 * h:2 * h + 2, :]), inc=("dcg", 16)))
    m_wqkv = m_wqkv[-1]
    m_wo = k.emit("gp", lambda e: e.dma_start(
        out=wo_sb[:, :, :], in_=wo.rearrange("(k p) f -> p k f", p=128)), inc=("dcg", 16))
    m_gw = k.emit("gp", lambda e: e.dma_start(
        out=gatew_sb[:, :, :], in_=gatew.rearrange("(k p) f -> p k f", p=128)), inc=("dcg", 16))

    conv_gu_store = [None] * NEXP
    conv_d_store = [None] * NEXP
    conv_gu_by_buf = {0: None, 1: None}
    conv_d_by_buf = {0: None, 1: None}

    def load_gu(e_i, buf, waits):
        waits = waits + later(conv_gu_by_buf[buf])
        if conv_gu_store[e_i] is None:
            g_src = swg if e_i == 0 else ewg[e_i - 1]
            u_src = swu if e_i == 0 else ewu[e_i - 1]
            dsem = "dcv%d" % buf
            k.emit("gp", lambda e, s=g_src, b=buf: e.dma_start(
                out=wg_sb[:, b, :, :], in_=s.rearrange("(k p) f -> p k f", p=128)),
                waits=waits, inc=(dsem, 16))
            lu = k.emit("gp", lambda e, s=u_src, b=buf: e.dma_start(
                out=wu_sb[:, b, :, :], in_=s.rearrange("(k p) f -> p k f", p=128)), inc=(dsem, 16))
            k.emit("sync", lambda e, b=buf, e_i=e_i: e.dma_start(
                out=scrg[e_i], in_=wg_sb[:, b, :, :].rearrange("p a b -> p (a b)")),
                waits=later(lu), inc=("dss", 16))
            stu = k.emit("sync", lambda e, b=buf, e_i=e_i: e.dma_start(
                out=scru[e_i], in_=wu_sb[:, b, :, :].rearrange("p a b -> p (a b)")),
                waits=later(lu), inc=("dss", 16))
            conv_gu_store[e_i] = stu
            conv_gu_by_buf[buf] = stu
            return lu
        conv_gu_by_buf[buf] = None
        sem = "dwg%d" % buf
        k.emit("act", lambda e, e_i=e_i, b=buf: e.dma_start(
            out=wg_sb[:, b, :, :].rearrange("p a b -> p (a b)"), in_=scrg[e_i]),
            waits=waits + later(conv_gu_store[e_i]), inc=(sem, 16))
        lu = k.emit("act", lambda e, e_i=e_i, b=buf: e.dma_start(
            out=wu_sb[:, b, :, :].rearrange("p a b -> p (a b)"), in_=scru[e_i]), inc=(sem, 16))
        return lu

    def load_d(e_i, buf, waits):
        waits = waits + later(conv_d_by_buf[buf])
        if conv_d_store[e_i] is None:
            d_src = swd if e_i == 0 else ewd[e_i - 1]
            ld = k.emit("gp", lambda e, s=d_src, b=buf: e.dma_start(
                out=wd_sb[:, b, :, :], in_=s.rearrange("(k p) f -> p k f", p=128)),
                waits=waits, inc=("dcd%d" % buf, 16))
            std = k.emit("sync", lambda e, b=buf, e_i=e_i: e.dma_start(
                out=scrd[e_i], in_=wd_sb[:, b, :, :].rearrange("p a b -> p (a b)")),
                waits=later(ld), inc=("dss", 16))
            conv_d_store[e_i] = std
            conv_d_by_buf[buf] = std
            return ld
        conv_d_by_buf[buf] = None
        ld = k.emit("sync", lambda e, e_i=e_i, b=buf: e.dma_start(
            out=wd_sb[:, b, :, :].rearrange("p a b -> p (a b)"), in_=scrd[e_i]),
            waits=waits + later(conv_d_store[e_i]), inc=("dwd%d" % buf, 16))
        return ld

    const_ready = later(*cst)
    prev_od = None
    prev_x2_mils = []
    prev_ytr = []
    prev_b4x3 = None
    exm_mil = {}
    tr_mil = {}
    pt_evac = {}
    pv_mil = {}
    wbuf_last_use = [None, None]
    wgu_loaded = [None, None]
    wd_loaded = [None, None]
    if not attn_only:
        wgu_loaded[0] = load_gu(0, 0, [])
        wd_loaded[0] = load_d(0, 0, [])
        wgu_loaded[1] = load_gu(1, 1, [])
        wd_loaded[1] = load_d(1, 1, [])

    def emit_head_x(sub_j):
        T0, NQ = subs[sub_j]
        return k.emit("sync", lambda e, T0=T0, NQ=NQ: e.dma_start(
            out=xa[:, 0:NQ, :],
            in_=xh[T0:T0 + 128 * NQ, :].rearrange("(q p) d -> p q d", p=128)),
            waits=later(*prev_x2_mils), inc=("dx", 16))

    def emit_head_q(xin, q):
        k.emit("dve", lambda e, q=q: e.bn_stats(bnst[:, 0, :], xa[:, q, 0:512]),
               waits=later(xin), drain_before=True)
        k.emit("dve", lambda e, q=q: e.bn_stats(bnst[:, 1, :], xa[:, q, 512:1024]))
        sm_ = k.emit("dve", lambda e, q=q: e.bn_aggr(
            mvs[:, 0, q, :], bnst.ap().rearrange("p a b -> p (a b)")),
            drain_before=True, inc=True)
        ln_m = k.emit("act", lambda e, q=q: e.activation(
            out=rstds[:, 0, q:q + 1], in_=mvs[:, 0, q, 1:2], func=AF.Ln,
            bias=epsc[:, 0:1], scale=1.0), waits=later(sm_, m_eps), inc=True)
        rs_m = k.emit("act", lambda e, q=q: e.activation(
            out=rstds[:, 0, q:q + 1], in_=rstds[:, 0, q:q + 1], func=AF.Exp, scale=-0.5),
            waits=later(ln_m), drain_before=True, inc=True)
        nm_ = k.emit("dve", lambda e, q=q: e.tensor_scalar(
            out=nmr[:, 0, q:q + 1], in0=mvs[:, 0, q, 0:1], scalar1=rstds[:, 0, q:q + 1],
            scalar2=-1.0, op0=ALU.mult, op1=ALU.mult),
            waits=later(rs_m), drain_before=True, inc=True)
        am = k.emit("gp", lambda e, q=q: e.tensor_scalar(
            out=z1_tm[:, q, :], in0=xa[:, q, :],
            scalar1=rstds[:, 0, q:q + 1], scalar2=nmr[:, 0, q:q + 1],
            op0=ALU.mult, op1=ALU.add),
            waits=later(nm_) + later(*prev_ytr), drain_before=True, inc=True)
        return am

    def emit_head(sub_j):
        xin = emit_head_x(sub_j)
        return [emit_head_q(xin, q) for q in range(subs[sub_j][1])]

    head_mils = emit_head(0)

    for sub_i, (T0, NQ) in enumerate(subs):
        ST = 128 * NQ
        NSEG = ST // 64

        def ln_stats_apply(src_view, dst_view, ready_by_q, stage, nq=NQ, extra_app_waits=()):
            app = []
            for q in range(nq):
                k.emit("dve", lambda e, q=q: e.bn_stats(bnst[:, 0, :], src_view[:, q, 0:512]),
                       waits=later(ready_by_q[min(q, len(ready_by_q) - 1)]), drain_before=True)
                k.emit("dve", lambda e, q=q: e.bn_stats(bnst[:, 1, :], src_view[:, q, 512:1024]))
                sm_ = k.emit("dve", lambda e, q=q, stage=stage: e.bn_aggr(
                    mvs[:, stage, q, :], bnst.ap().rearrange("p a b -> p (a b)")),
                    drain_before=True, inc=True)
                ln_m = k.emit("act", lambda e, q=q, stage=stage: e.activation(
                    out=rstds[:, stage, q:q + 1], in_=mvs[:, stage, q, 1:2], func=AF.Ln,
                    bias=epsc[:, 0:1], scale=1.0), waits=later(sm_, m_eps), inc=True)
                rs_m = k.emit("act", lambda e, q=q, stage=stage: e.activation(
                    out=rstds[:, stage, q:q + 1], in_=rstds[:, stage, q:q + 1],
                    func=AF.Exp, scale=-0.5), waits=later(ln_m), drain_before=True, inc=True)
                nm_ = k.emit("dve", lambda e, q=q, stage=stage: e.tensor_scalar(
                    out=nmr[:, stage, q:q + 1], in0=mvs[:, stage, q, 0:1],
                    scalar1=rstds[:, stage, q:q + 1], scalar2=-1.0,
                    op0=ALU.mult, op1=ALU.mult),
                    waits=later(rs_m), drain_before=True, inc=True)
                am = k.emit("gp", lambda e, q=q, stage=stage: e.tensor_scalar(
                    out=dst_view[:, q, :], in0=src_view[:, q, :],
                    scalar1=rstds[:, stage, q:q + 1], scalar2=nmr[:, stage, q:q + 1],
                    op0=ALU.mult, op1=ALU.add),
                    waits=later(nm_) + list(extra_app_waits), drain_before=True, inc=True)
                app.append(am)
            return app

        z1_mils = head_mils

        def transpose_tm_to_fm(src_tm, dst_fm, src_mils, nq=NQ, ret_tr=False):
            evac = []
            trs = []
            g = 0
            for q in range(nq):
                for ch in range(2):
                    b = g % 2
                    ws = later(src_mils[q]) + bank_write_waits(b) + const_ready
                    def tr(e, q=q, ch=ch, b=b):
                        last = None
                        for cc in range(4):
                            c = 4 * ch + cc
                            last = e.transpose(PSB3[:, b, 128 * cc:128 * cc + 128],
                                               src_tm[:, q, 128 * c:128 * c + 128], ids[:, :])
                        return last
                    tm_ = k.pe("tr", tr, waits=ws, inc=True)
                    bank_set_w(b, tm_)
                    trs.append(tm_)
                    def ev(e, q=q, ch=ch, b=b):
                        dst = dst_fm[:, 4 * ch:4 * ch + 4, 128 * q:128 * q + 128]
                        return e.activation(
                            out=dst, in_=PSB3[:, b, 0:512].rearrange("p (a c) -> p a c", c=128),
                            func=AF.Identity, bias=0.0, scale=1.0)
                    em_ = k.emit("act", ev, waits=later(tm_), inc=True)
                    bank_read(b, em_)
                    evac.append(em_)
                    g += 1
            if ret_tr:
                return evac, trs
            return evac

        z1fm_mils = transpose_tm_to_fm(z1_tm, z1_fm, z1_mils)
        z1fm_all = later(z1fm_mils[-1], *z1fm_mils)

        def qkcols(kc, c, which):
            off = 1024 * which + 128 * c
            return wqkv_sb[:, kc, off:off + 128]

        qk_mils = []
        gi = 0
        for c in range(8):
            for which in range(2):
                b = 2 + gi % 4
                ws = z1fm_all + bank_write_waits(b) + later(m_wqkv)
                def mmq(e, c=c, b=b, which=which, ST=ST):
                    last = None
                    for kc in range(8):
                        last = e.matmul(PS[:, b, 0:ST], qkcols(kc, c, which),
                                        z1_fm[:, kc, 0:ST], start=(kc == 0), stop=(kc == 7))
                    return last
                pm = k.pe("f", mmq, waits=ws, inc=True)
                bank_set_w(b, pm)
                dst = qf if which == 0 else kf
                if which == 0:
                    em = k.emit("act", lambda e, c=c, b=b, dst=dst, ST=ST: e.activation(
                        out=dst[:, c, :ST], in_=PS[:, b, 0:ST], func=AF.Identity,
                        bias=bqk8[:, c:c + 1], scale=0.125),
                        waits=later(pm, m_bqk8), inc=True)
                else:
                    em = k.emit("act", lambda e, c=c, b=b, dst=dst, ST=ST: e.activation(
                        out=dst[:, c, :ST], in_=PS[:, b, 0:ST], func=AF.Identity,
                        bias=bqk_sb[:, 8 + c:8 + c + 1], scale=1.0),
                        waits=later(pm), inc=True)
                bank_read(b, em)
                qk_mils.append(em)
                gi += 1
        qk_ready = later(*qk_mils)

        v_mils = []
        for q in range(NQ):
            for half in range(2):
                b = 2 + gi % 4
                ws = z1fm_all + bank_write_waits(b) + later(m_wqkv)
                def mmv(e, q=q, half=half, b=b):
                    last = None
                    for kc in range(8):
                        rhs = wqkv_sb[:, kc, 2048 + 512 * half:2048 + 512 * half + 512]
                        last = e.matmul(PS[:, b, :], z1_fm[:, kc, 128 * q:128 * q + 128], rhs,
                                        start=(kc == 0), stop=(kc == 7))
                    return last
                pm = k.pe("f", mmv, waits=ws, inc=True)
                bank_set_w(b, pm)
                em = k.emit("dve", lambda e, q=q, half=half, b=b: e.scalar_tensor_tensor(
                    out=vt[:, q, 512 * half:512 * half + 512], in0=PS[:, b, :],
                    scalar=1.0, in1=bvb[:, 512 * half:512 * half + 512],
                    op0=ALU.mult, op1=ALU.add),
                    waits=later(pm) + (later(prev_b4x3) if (q == 0 and half == 0) else []),
                    inc=True, drain_before=(q == 0 and half == 0))
                bank_read(b, em)
                v_mils.append(em)
                gi += 1
        v_ready = later(*v_mils)

        of_mils = []
        S_banks = [2, 3, 6, 7]
        s_mil = {}
        p_mil = {}

        def emit_S(c):
            b = S_banks[c % 4]
            ws = qk_ready + bank_write_waits(b)
            def mms(e, c=c, b=b, NSEG=NSEG):
                last = None
                for s in range(NSEG):
                    e.matmul(PS[0:64, b, 64 * s:64 * s + 64],
                             qf[0:64, c, 64 * s:64 * s + 64], kf[0:64, c, 64 * s:64 * s + 64],
                             start=True, stop=True, tile_position=(0, 0))
                    last = e.matmul(PS[64:128, b, 64 * s:64 * s + 64],
                                    qf[64:128, c, 64 * s:64 * s + 64], kf[64:128, c, 64 * s:64 * s + 64],
                                    start=True, stop=True, tile_position=(64, 64))
                return last
            pm = k.pe("q64", mms, waits=ws, inc=True)
            bank_set_w(b, pm)
            s_mil[c] = pm

        def emit_softmax(c):
            ci = c % 4
            b = S_banks[c % 4]
            ws = later(s_mil[c]) + later(exm_mil.get((c - 4) % 8))
            rm = k.emit("dve", lambda e, ci=ci, b=b, NSEG=NSEG, ST=ST: e.tensor_reduce(
                nmx[:, ci, 0:NSEG], PS[:, b, 0:ST].rearrange("p (s t) -> p s t", t=64),
                axis=AX.X, op=ALU.max, negate=True), waits=ws, drain_before=True, inc=True)
            exm = None
            psb_war = later(tr_mil.get((c - 4) % 8))
            for s in range(NSEG):
                exm = k.emit("act", lambda e, ci=ci, b=b, s=s: e.activation(
                    out=psb[:, ci, 64 * s:64 * s + 64], in_=PS[:, b, 64 * s:64 * s + 64],
                    func=AF.Exp, bias=nmx[:, ci, s:s + 1], scale=1.0),
                    waits=later(rm) + (psb_war if s == 0 else []),
                    inc=(True if s == NSEG - 1 else None))
            exm_mil[c] = exm
            bank_read(b, exm)
            sm_ = k.emit("dve", lambda e, ci=ci, NSEG=NSEG, ST=ST: e.tensor_reduce(
                ssum[:, ci, 0:NSEG], psb[:, ci, 0:ST].rearrange("p (s t) -> p s t", t=64),
                axis=AX.X, op=ALU.add), waits=later(exm), inc=True)
            rc = k.emit("dve", lambda e, ci=ci, NSEG=NSEG: e.reciprocal(
                rr[:, ci, 0:NSEG], ssum[:, ci, 0:NSEG]), waits=later(sm_),
                drain_before=True, inc=True)
            pm_ = None
            for s in range(NSEG):
                pm_ = k.emit("dve", lambda e, ci=ci, s=s: e.tensor_scalar(
                    out=psb[:, ci, 64 * s:64 * s + 64], in0=psb[:, ci, 64 * s:64 * s + 64],
                    scalar1=rr[:, ci, s:s + 1], scalar2=None, op0=ALU.mult, op1=ALU.bypass),
                    waits=later(rc), drain_before=(s == 0), inc=(True if s == NSEG - 1 else None))
            p_mil[c] = pm_

        def emit_T_pe(c):
            psrc = c % 4
            b = c % 2
            ws = later(p_mil[c]) + bank_write_waits(b)
            def trp(e, psrc=psrc, b=b, NQ=NQ):
                last = None
                for blk in range(NQ):
                    last = e.transpose(PSB3[:, b, 128 * blk:128 * blk + 128],
                                       psb[:, psrc, 128 * blk:128 * blk + 128], ids[:, :])
                return last
            tm_ = k.pe("tr", trp, waits=ws, inc=True)
            bank_set_w(b, tm_)
            tr_mil[c] = tm_

        def emit_T_evac(c):
            pci = c % 2
            b = c % 2
            ws2 = later(tr_mil[c]) + later(pv_mil.get((c - 2) % 8))
            em = k.emit("dve", lambda e, pci=pci, b=b, ST=ST: e.tensor_copy(
                pts[:, pci, 0:ST], PSB3[:, b, 0:ST]), waits=ws2, inc=True)
            bank_read(b, em)
            pt_evac[c] = em

        def mpv(e, c, ci, b, parity, NSEG):
            last = None
            for s in range(parity, NSEG, 2):
                sp = s % 2
                tq = s // 2
                for hp in range(2):
                    last = e.matmul(
                        PS[64 * hp:64 * hp + 64, b, 64 * s:64 * s + 64],
                        vt[64 * sp:64 * sp + 64, tq, 128 * c + 64 * hp:128 * c + 64 * hp + 64],
                        pts[64 * sp:64 * sp + 64, ci,
                            128 * (s // 2) + 64 * hp:128 * (s // 2) + 64 * hp + 64],
                        start=True, stop=True, tile_position=(64 * sp, 64 * hp))
            return last

        def emit_PV_parity(cs, parity):
            last = {}
            for c in cs:
                ci = c % 2
                b = 4 + ci
                ws = later(pt_evac[c]) + v_ready
                if parity == 0:
                    ws += bank_write_waits(b)
                last[c] = k.pe("pv%d" % parity,
                               lambda e, c=c, ci=ci, b=b, parity=parity, NSEG=NSEG: mpv(
                                   e, c, ci, b, parity, NSEG),
                               waits=ws, inc=(True if parity == 1 else None))
            return last

        def emit_O_evac(c):
            b = 4 + c % 2
            pvm = pv_mil[c]
            bank_set_w(b, pvm)
            em = k.emit("act", lambda e, c=c, b=b, ST=ST: e.activation(
                out=of[:, c, 0:ST], in_=PS[:, b, 0:ST], func=AF.Identity, bias=0.0, scale=1.0),
                waits=later(pvm), inc=True)
            bank_read(b, em)
            of_mils.append(em)

        emit_S(0)
        emit_softmax(0)
        emit_S(1)
        emit_softmax(1)
        emit_S(2)
        emit_S(3)
        for pr in range(4):
            c0, c1 = 2 * pr, 2 * pr + 1
            emit_T_pe(c0)
            emit_T_pe(c1)
            emit_T_evac(c0)
            emit_T_evac(c1)
            for cn in (c0 + 2, c1 + 2):
                if cn < 8 and cn >= 4:
                    pass
            if c0 + 2 < 8 and c0 + 2 >= 2:
                if c0 + 2 >= 2 and (c0 + 2) not in p_mil:
                    emit_softmax(c0 + 2)
                if c1 + 2 < 8 and (c1 + 2) not in p_mil:
                    emit_softmax(c1 + 2)
            emit_PV_parity((c0, c1), 0)
            pv2 = emit_PV_parity((c0, c1), 1)
            pv_mil[c0] = pv2[c0]
            pv_mil[c1] = pv2[c1]
            emit_O_evac(c0)
            emit_O_evac(c1)
            for cn in (2 * pr + 4, 2 * pr + 5):
                if cn < 8:
                    emit_S(cn)
        of_ready = later(*of_mils)

        x2_mils = []
        rs3_mils = []
        prev_resadd = None
        for q in range(NQ):
            bpair = (6, 7) if q % 2 == 0 else (2, 3)
            ap_mils = []
            for jh in range(2):
                b = bpair[jh]
                ws = of_ready + bank_write_waits(b) + later(m_wo)
                def mmo(e, q=q, jh=jh, b=b):
                    last = None
                    for c in range(8):
                        last = e.matmul(PS[:, b, :], of[:, c, 128 * q:128 * q + 128],
                                        wo_sb[:, c, 512 * jh:512 * jh + 512],
                                        start=(c == 0), stop=(c == 7))
                    return last
                pm = k.pe("f", mmo, waits=ws, inc=True)
                bank_set_w(b, pm)
                em = k.emit("dve", lambda e, jh=jh, b=b: e.scalar_tensor_tensor(
                    out=aps[:, 0, 512 * jh:512 * jh + 512], in0=PS[:, b, :], scalar=1.0,
                    in1=bob[:, 512 * jh:512 * jh + 512], op0=ALU.mult, op1=ALU.add),
                    waits=later(pm), drain_before=(jh == 0), inc=True)
                bank_read(b, em)
                ap_mils.append(em)
            k.emit("dve", lambda e: e.bn_stats(bnst[:, 0, :], aps[:, 0, 0:512]),
                   waits=later(*ap_mils), drain_before=True)
            k.emit("dve", lambda e: e.bn_stats(bnst[:, 1, :], aps[:, 0, 512:1024]))
            ag = k.emit("dve", lambda e, q=q: e.bn_aggr(
                mvs[:, 1, q, :], bnst.ap().rearrange("p a b -> p (a b)")), drain_before=True, inc=True)
            ln_m = k.emit("act", lambda e, q=q: e.activation(
                out=rstds[:, 1, q:q + 1], in_=mvs[:, 1, q, 1:2], func=AF.Ln, bias=epsc[:, 0:1], scale=1.0),
                waits=later(ag, m_eps), inc=True)
            rs_m = k.emit("act", lambda e, q=q: e.activation(
                out=rstds[:, 1, q:q + 1], in_=rstds[:, 1, q:q + 1], func=AF.Exp, scale=-0.5),
                waits=later(ln_m), drain_before=True, inc=True)
            ts_m = k.emit("dve", lambda e, q=q: e.tensor_scalar(
                out=z2t[:, 0, :], in0=aps[:, 0, :], scalar1=mvs[:, 1, q, 0:1],
                scalar2=rstds[:, 1, q:q + 1], op0=ALU.subtract, op1=ALU.mult),
                waits=later(rs_m, prev_resadd), drain_before=True, inc=True)
            x2m = k.emit("dve", lambda e, q=q: e.tensor_tensor(
                out=x_tm[:, q, :], in0=xa[:, q, :], in1=z2t[:, 0, :], op=ALU.add),
                waits=later(ts_m) + [("do", k.ct["do"])], drain_before=True, inc=True)
            prev_resadd = x2m
            x2_mils.append(x2m)
            if not attn_only:
                k.emit("dve", lambda e, q=q: e.bn_stats(bnst[:, 0, :], x_tm[:, q, 0:512]),
                       waits=later(x2m), drain_before=True)
                k.emit("dve", lambda e, q=q: e.bn_stats(bnst[:, 1, :], x_tm[:, q, 512:1024]))
                ag3 = k.emit("dve", lambda e, q=q: e.bn_aggr(
                    mvs[:, 2, q, :], bnst.ap().rearrange("p a b -> p (a b)")),
                    drain_before=True, inc=True)
                ln3 = k.emit("act", lambda e, q=q: e.activation(
                    out=rstds[:, 2, q:q + 1], in_=mvs[:, 2, q, 1:2], func=AF.Ln,
                    bias=epsc[:, 0:1], scale=1.0), waits=later(ag3, m_eps), inc=True)
                rs3 = k.emit("act", lambda e, q=q: e.activation(
                    out=rstds[:, 2, q:q + 1], in_=rstds[:, 2, q:q + 1],
                    func=AF.Exp, scale=-0.5), waits=later(ln3), drain_before=True, inc=True)
                rs3_mils.append(rs3)

        if attn_only:
            od = k.emit("sync", lambda e, T0=T0, NQ=NQ: e.dma_start(
                out=outh[T0:T0 + 128 * NQ, :].rearrange("(q p) d -> p q d", p=128),
                in_=x_tm[:, 0:NQ, :]), waits=later(*x2_mils), inc=("do", 16))
            prev_x2_mils = x2_mils
            prev_ytr = []
            if sub_i + 1 < len(subs):
                head_mils = emit_head(sub_i + 1)
            continue

        y_mils = []
        for q in range(NQ):
            nm3 = k.emit("dve", lambda e, q=q: e.tensor_scalar(
                out=nmr[:, 2, q:q + 1], in0=mvs[:, 2, q, 0:1],
                scalar1=rstds[:, 2, q:q + 1], scalar2=-1.0,
                op0=ALU.mult, op1=ALU.mult),
                waits=later(rs3_mils[q]), drain_before=True, inc=True)
            y_mils.append(k.emit("gp", lambda e, q=q: e.tensor_scalar(
                out=z1_tm[:, q, :], in0=x_tm[:, q, :],
                scalar1=rstds[:, 2, q:q + 1], scalar2=nmr[:, 2, q:q + 1],
                op0=ALU.mult, op1=ALU.add),
                waits=later(nm3), drain_before=True, inc=True))
        yfm_mils, ytr_mils = transpose_tm_to_fm(z1_tm, z1_fm, y_mils, ret_tr=True)
        yfm_all = later(yfm_mils[-1], *yfm_mils)

        sig_mils = []
        for q in range(NQ):
            b = 6 + q % 2
            ws = yfm_all + bank_write_waits(b) + later(m_gw)
            def mmg(e, q=q, b=b):
                last = None
                for kc in range(8):
                    last = e.matmul(PS[:, b, 0:NE], z1_fm[:, kc, 128 * q:128 * q + 128],
                                    gatew_sb[:, kc, :], start=(kc == 0), stop=(kc == 7))
                return last
            pm = k.pe("f", mmg, waits=ws, inc=True)
            bank_set_w(b, pm)
            em = k.emit("dve", lambda e, q=q, b=b: e.scalar_tensor_tensor(
                out=gl[:, q, :], in0=PS[:, b, 0:NE], scalar=1.0, in1=gbb[:, :],
                op0=ALU.mult, op1=ALU.add), waits=later(pm), inc=True)
            bank_read(b, em)
            ex = k.emit("act", lambda e, q=q: e.activation(
                out=ge[:, q, :], in_=gl[:, q, :], func=AF.Exp, scale=-1.0),
                waits=later(em), inc=True)
            t1 = k.emit("dve", lambda e, q=q: e.tensor_scalar(
                out=ge[:, q, :], in0=ge[:, q, :], scalar1=1.0, scalar2=None,
                op0=ALU.add, op1=ALU.bypass), waits=later(ex), inc=True)
            sg = k.emit("dve", lambda e, q=q: e.reciprocal(gl[:, q, :], ge[:, q, :]),
                        waits=later(t1), drain_before=True, inc=True)
            sig_mils.append(sg)

        wsb_mils = []
        for q in range(NQ):
            k.emit("dve", lambda e, q=q: e.tensor_reduce(
                m1[:, q:q + 1], gl[:, q, :], axis=AX.X, op=ALU.max),
                waits=later(sig_mils[q]), drain_before=True)
            k.emit("dve", lambda e, q=q: e.tensor_scalar(
                out=eqm[:, q, :], in0=gl[:, q, :], scalar1=m1[:, q:q + 1], scalar2=-1e30,
                op0=ALU.is_ge, op1=ALU.mult), drain_before=True)
            k.emit("dve", lambda e, q=q: e.tensor_tensor(
                out=eqm[:, q, :], in0=gl[:, q, :], in1=eqm[:, q, :], op=ALU.add), drain_before=True)
            k.emit("dve", lambda e, q=q: e.tensor_reduce(
                m2[:, q:q + 1], eqm[:, q, :], axis=AX.X, op=ALU.max), drain_before=True)
            k.emit("dve", lambda e, q=q: e.tensor_tensor(
                out=den[:, q:q + 1], in0=m1[:, q:q + 1], in1=m2[:, q:q + 1], op=ALU.add),
                drain_before=True)
            k.emit("dve", lambda e, q=q: e.reciprocal(den[:, q:q + 1], den[:, q:q + 1]),
                   drain_before=True)
            k.emit("dve", lambda e, q=q: e.tensor_scalar(
                out=eqm[:, q, :], in0=gl[:, q, :], scalar1=m2[:, q:q + 1], scalar2=None,
                op0=ALU.is_ge, op1=ALU.bypass), drain_before=True)
            k.emit("dve", lambda e, q=q: e.tensor_tensor(
                out=w_tm[:, q, :], in0=gl[:, q, :], in1=eqm[:, q, :], op=ALU.mult),
                drain_before=True)
            k.emit("dve", lambda e, q=q: e.tensor_scalar(
                out=w_tm[:, q, :], in0=w_tm[:, q, :], scalar1=den[:, q:q + 1], scalar2=None,
                op0=ALU.mult, op1=ALU.bypass), drain_before=True)
            wm = k.emit("dve", lambda e, q=q: e.tensor_copy(wsb[:, q, :], w_tm[:, q, :]),
                        drain_before=True, inc=True)
            wsb_mils.append(wm)
        h_prev = None
        h_mil = [None] * NEXP
        down_mil = [None] * NEXP
        up_last = [None] * NEXP
        wd_mil = [None] * NEXP
        last_down = {}
        acc_banks = {}
        acc_first = {}
        acc_war = {}

        def emit_ups(eidx):
            nonlocal h_prev
            wb_i = eidx % 2
            wm = wgu_loaded[wb_i]
            ev_g = []
            for i in range(4):
                b = i % 2
                ws = yfm_all + bank_write_waits(b) + later(wm)
                def mmug(e, i=i, b=b, wb_i=wb_i, ST=ST):
                    last = None
                    for kc in range(8):
                        last = e.matmul(PS[:, b, 0:ST], wg_sb[:, wb_i, kc, 128 * i:128 * i + 128],
                                        z1_fm[:, kc, 0:ST], start=(kc == 0), stop=(kc == 7))
                    return last
                pm = k.pe("f", mmug, waits=ws, inc=True)
                bank_set_w(b, pm)
                em = k.emit("act", lambda e, i=i, b=b, hp_=eidx % 2, ST=ST: e.activation(
                    out=gs[:, hp_, i, 0:ST], in_=PS[:, b, 0:ST], func=AF.Silu, bias=0.0, scale=1.0),
                    waits=later(pm, h_mil[eidx - 2] if eidx >= 2 else None), inc=True)
                bank_read(b, em)
                ev_g.append(em)
            ev_u = []
            for i in range(4):
                b = i % 2
                ws = yfm_all + bank_write_waits(b) + later(wm)
                def mmuu(e, i=i, b=b, wb_i=wb_i, ST=ST):
                    last = None
                    for kc in range(8):
                        last = e.matmul(PS[:, b, 0:ST], wu_sb[:, wb_i, kc, 128 * i:128 * i + 128],
                                        z1_fm[:, kc, 0:ST], start=(kc == 0), stop=(kc == 7))
                    return last
                pm = k.pe("f", mmuu, waits=ws, inc=True)
                bank_set_w(b, pm)
                up_last[eidx] = pm
                war_u = h_mil[eidx - 2] if eidx >= 2 else None
                if eidx == 0:
                    em = k.emit("dve", lambda e, i=i, b=b, hp_=eidx % 2, ST=ST: e.tensor_copy(
                        us[:, hp_, i, 0:ST], PS[:, b, 0:ST]), waits=later(pm, war_u), inc=True)
                else:
                    em = k.emit("dve", lambda e, i=i, b=b, hp_=eidx % 2, eidx=eidx, ST=ST: e.tensor_tensor(
                        out=us[:, hp_, i, 0:ST], in0=PS[:, b, 0:ST], in1=wbc[:, eidx - 1, 0:ST],
                        op=ALU.mult),
                        waits=later(pm, war_u, wbc_mils[eidx - 1]), inc=True)
                bank_read(b, em)
                ev_u.append(em)
            h_m = None
            hp_ = eidx % 2
            war = later(down_mil[eidx - 2]) if eidx >= 2 else []
            for i in range(4):
                h_m = k.emit("dve", lambda e, i=i, hp_=hp_, ST=ST: e.tensor_tensor(
                    out=hs[:, hp_, i, 0:ST], in0=gs[:, hp_, i, 0:ST], in1=us[:, hp_, i, 0:ST],
                    op=ALU.mult),
                    waits=later(ev_g[i], ev_u[i]) + (war if i == 0 else []),
                    drain_before=(i == 0), inc=(True if i == 3 else None))
            h_mil[eidx] = h_m
            h_prev = h_m

        def emit_downs(eidx):
            wb_i = eidx % 2
            hp_ = eidx % 2
            dm = None
            wdm = wd_loaded[wb_i]
            for tq in range(NQ):
                for jh in range(2):
                    bb = acc_banks[(tq, jh)]
                    ws = later(h_mil[eidx], wdm) + (acc_war[bb] if acc_first[bb] else [])
                    def mmd(e, tq=tq, jh=jh, bb=bb, wb_i=wb_i, hp_=hp_,
                            first=acc_first[bb], eidx=eidx):
                        last = None
                        for i in range(4):
                            last = e.matmul(PS[:, bb, :], hs[:, hp_, i, 128 * tq:128 * tq + 128],
                                            wd_sb[:, wb_i, i, 512 * jh:512 * jh + 512],
                                            start=(first and i == 0),
                                            stop=(eidx == NEXP - 1 and i == 3))
                        return last
                    dm = k.pe("f", mmd, waits=ws,
                              inc=(True if (tq == NQ - 1 and jh == 1) or eidx == NEXP - 1 else None))
                    if eidx == NEXP - 1:
                        bank_set_w(bb, dm)
                        last_down[(tq, jh)] = dm
                    acc_first[bb] = False
            down_mil[eidx] = dm

        prev_x2_mils = x2_mils
        prev_ytr = ytr_mils
        wbc_mils = []
        for eidx in range(NEXP):
            emit_ups(eidx)
            if eidx == 0:
                b = 0
                ws = later(*wsb_mils) + bank_write_waits(b)
                def trw(e, b=b, NQ=NQ):
                    last = None
                    for q in range(NQ):
                        last = e.transpose(PSB3[0:16, b, 128 * q:128 * q + 128],
                                           wsb[:, q, :], ids[:, :])
                    return last
                tw = k.pe("tr", trw, waits=ws, inc=True)
                bank_set_w(b, tw)
                ew = k.emit("dve", lambda e, b=b, ST=ST: e.tensor_copy(
                    wfm[0:16, 0:ST], PSB3[0:16, b, 0:ST]), waits=later(tw), inc=True)
                bank_read(b, ew)
                del wbc_mils[:]
                slabA_free = later(*[m for m in of_mils])
                for ei2 in range(NE):
                    b = 6 + ei2 % 2
                    ws = later(ew) + bank_write_waits(b) + (slabA_free if ei2 < 2 else [])
                    pm = k.pe("k32", lambda e, ei2=ei2, b=b, ST=ST: e.matmul(
                        PS[:, b, 0:ST], eall[0:16, 128 * ei2:128 * ei2 + 128], wfm[0:16, 0:ST],
                        start=True, stop=True), waits=ws, inc=True)
                    bank_set_w(b, pm)
                    em = k.emit("act", lambda e, ei2=ei2, b=b, ST=ST: e.activation(
                        out=wbc[:, ei2, 0:ST], in_=PS[:, b, 0:ST], func=AF.Identity,
                        bias=0.0, scale=1.0), waits=later(pm), inc=True)
                    bank_read(b, em)
                    wbc_mils.append(em)
                wbc_ready = later(*wbc_mils)
                acc_banks.clear(); acc_first.clear(); acc_war.clear()
                for tq in range(NQ):
                    for jh in range(2):
                        bb = 2 + 2 * tq + jh
                        acc_banks[(tq, jh)] = bb
                for bb in set(acc_banks.values()):
                    acc_first[bb] = True
                    acc_war[bb] = bank_write_waits(bb)
            if eidx + 2 < NEXP:
                wgu_loaded[eidx % 2] = load_gu(eidx + 2, eidx % 2, later(up_last[eidx]))
            if eidx > 0:
                emit_downs(eidx - 1)
                if eidx + 1 < NEXP:
                    wd_loaded[(eidx + 1) % 2] = load_d(eidx + 1, (eidx + 1) % 2,
                                                       later(down_mil[eidx - 1]))
            if sub_i + 1 < len(subs):
                if eidx == 8:
                    head_xin = emit_head_x(sub_i + 1)
                    head_mils = []
                elif eidx in (10, 12, 14):
                    q = (eidx - 10) // 2
                    if q < subs[sub_i + 1][1]:
                        head_mils.append(emit_head_q(head_xin, q))
        emit_downs(NEXP - 1)
        if sub_i + 1 < len(subs):
            wgu_loaded[0] = load_gu(0, 0, later(up_last[NEXP - 1]))
            wgu_loaded[1] = load_gu(1, 1, later(up_last[NEXP - 2]))
            wd_loaded[1] = load_d(1, 1, later(down_mil[NEXP - 2]))
            wd_loaded[0] = load_d(0, 0, later(down_mil[NEXP - 1]))
        if len(subs) == 1 or (sub_i + 1 == len(subs)):
            pass

        ffn_done = []
        prev_x3 = prev_resadd
        for tq in range(NQ):
            yrc = k.emit("dve", lambda e, tq=tq: e.tensor_scalar(
                out=z2t[:, 0, :], in0=x_tm[:, tq, :], scalar1=mvs[:, 2, tq, 0:1],
                scalar2=rstds[:, 2, tq:tq + 1], op0=ALU.subtract, op1=ALU.mult),
                waits=later(last_down[(tq, 0)], prev_x3), drain_before=True, inc=True)
            apm = []
            for jh in range(2):
                bb = acc_banks[(tq, jh)]
                em = k.emit("dve", lambda e, tq=tq, jh=jh, bb=bb: e.scalar_tensor_tensor(
                    out=aps[:, 0, 512 * jh:512 * jh + 512], in0=PS[:, bb, :], scalar=1.0,
                    in1=z2t[:, 0, 512 * jh:512 * jh + 512], op0=ALU.mult, op1=ALU.add),
                    waits=later(last_down[(tq, jh)], yrc), drain_before=True, inc=True)
                bank_read(bb, em)
                apm.append(em)
            k.emit("dve", lambda e: e.bn_stats(bnst[:, 0, :], aps[:, 0, 0:512]),
                   waits=later(*apm), drain_before=True)
            k.emit("dve", lambda e: e.bn_stats(bnst[:, 1, :], aps[:, 0, 512:1024]))
            ag = k.emit("dve", lambda e, tq=tq: e.bn_aggr(
                mvs[:, 3, tq, :], bnst.ap().rearrange("p a b -> p (a b)")), drain_before=True, inc=True)
            ln_m = k.emit("act", lambda e, tq=tq: e.activation(
                out=rstds[:, 3, tq:tq + 1], in_=mvs[:, 3, tq, 1:2], func=AF.Ln, bias=epsc[:, 0:1], scale=1.0),
                waits=later(ag, m_eps), inc=True)
            rs_m = k.emit("act", lambda e, tq=tq: e.activation(
                out=rstds[:, 3, tq:tq + 1], in_=rstds[:, 3, tq:tq + 1], func=AF.Exp, scale=-0.5),
                waits=later(ln_m), drain_before=True, inc=True)
            ts_m = k.emit("dve", lambda e, tq=tq: e.tensor_scalar(
                out=z2t[:, 0, :], in0=aps[:, 0, :], scalar1=mvs[:, 3, tq, 0:1],
                scalar2=rstds[:, 3, tq:tq + 1], op0=ALU.subtract, op1=ALU.mult),
                waits=later(rs_m), drain_before=True, inc=True)
            x3m = k.emit("dve", lambda e, tq=tq: e.tensor_tensor(
                out=x_tm[:, tq, :], in0=x_tm[:, tq, :], in1=z2t[:, 0, :], op=ALU.add),
                waits=later(ts_m), drain_before=True, inc=True)
            prev_x3 = x3m
            prev_b4x3 = x3m
            ffn_done.append(x3m)
        od = k.emit("sync", lambda e, T0=T0, NQ=NQ: e.dma_start(
            out=outh[T0:T0 + 128 * NQ, :].rearrange("(q p) d -> p q d", p=128),
            in_=x_tm[:, 0:NQ, :]), waits=later(*ffn_done), inc=("do", 16))

    k.emit("sync", lambda e: None, waits=[(s, k.ct[s]) for s in
                                          ("do", "dx", "dwg0", "dwg1", "dwd0", "dwd1", "dc", "dcg", "dcv0", "dcv1", "dcd0", "dcd1", "dss")])
    k.replay(ctx)
    ctx.close()
    return nc


from concourse.bass_utils import run_bass_kernel_spmd


def _permute_wqkv(w_qkv):
    w = np.asarray(w_qkv, np.float32)
    idx = np.zeros(3 * D, np.int64)
    for c in range(8):
        for j in range(128):
            h = 2 * c + (1 if j >= 64 else 0)
            m = j % 64
            idx[128 * c + j] = h * 192 + m
            idx[1024 + 128 * c + j] = h * 192 + 64 + m
    for h in range(16):
        for m in range(64):
            idx[2048 + 64 * h + m] = h * 192 + 128 + m
    return np.ascontiguousarray(w[:, idx])


def _host_layout(b_qkv, b_o, gate_b):
    bqk = np.zeros((128, 16), np.float32)
    b_qkv = np.asarray(b_qkv, np.float32)
    for c in range(8):
        for p in range(128):
            h = 2 * c + (1 if p >= 64 else 0)
            m = p % 64
            bqk[p, c] = b_qkv[h * 192 + m]
            bqk[p, 8 + c] = b_qkv[h * 192 + 64 + m]
    bvrow = np.zeros((1, D), np.float32)
    for h in range(16):
        bvrow[0, 64 * h:64 * h + 64] = b_qkv[h * 192 + 128:h * 192 + 192]
    eallv = np.zeros((16, 2048), np.float32)
    for e in range(16):
        eallv[e, 128 * e:128 * (e + 1)] = 1.0
    return dict(
        bqk=bqk, bvrow=bvrow,
        borow=np.asarray(b_o, np.float32).reshape(1, D),
        gbrow=np.asarray(gate_b, np.float32).reshape(1, NE),
        identh=np.eye(128).astype(NPHALF),
        eallh=eallv.astype(NPHALF),
    )


def kernel(L0_attn_x, L0_attn_w_qkv, L0_attn_b_qkv, L0_attn_w_o, L0_attn_b_o,
           L0_ffn_gate_w, L0_ffn_gate_b, L0_ffn_shared_wg, L0_ffn_shared_wu,
           L0_ffn_shared_wd, L0_ffn_experts_wg, L0_ffn_experts_wu, L0_ffn_experts_wd):
    x = np.ascontiguousarray(np.asarray(L0_attn_x, np.float32))
    B, S, Dm = x.shape
    assert (B, S, Dm) == (8, 4096, 1024)
    consts = _host_layout(L0_attn_b_qkv, L0_attn_b_o, L0_ffn_gate_b)
    shared = dict(
        wqkv=_permute_wqkv(L0_attn_w_qkv),
        wo=np.ascontiguousarray(np.asarray(L0_attn_w_o, np.float32)),
        gatew=np.ascontiguousarray(np.asarray(L0_ffn_gate_w, np.float32)),
        swg=np.ascontiguousarray(np.asarray(L0_ffn_shared_wg, np.float32)),
        swu=np.ascontiguousarray(np.asarray(L0_ffn_shared_wu, np.float32)),
        swd=np.ascontiguousarray(np.asarray(L0_ffn_shared_wd, np.float32)),
        ewg=np.ascontiguousarray(np.asarray(L0_ffn_experts_wg, np.float32)),
        ewu=np.ascontiguousarray(np.asarray(L0_ffn_experts_wu, np.float32)),
        ewd=np.ascontiguousarray(np.asarray(L0_ffn_experts_wd, np.float32)),
        **consts)
    nc = build(S)
    in_maps = [dict(x=x[c], **shared) for c in range(8)]
    res = run_bass_kernel_spmd(nc, in_maps, core_ids=list(range(8)))
    return np.stack([r["out"] for r in res.results], axis=0)
